# version 02 (final)
# baseline (speedup 1.0000x reference)
import math
import numpy as np
from contextlib import ExitStack

import concourse.bass as bass
import concourse.mybir as mybir
from concourse.bass_utils import run_bass_kernel_spmd

F32 = mybir.dt.float32
BF16 = mybir.dt.bfloat16
I32 = mybir.dt.int32
F16 = mybir.dt.float16
AF = mybir.ActivationFunctionType
ALU = mybir.AluOpType
AX = mybir.AxisListType

NCORES = 8
B, S, H = 4, 4096, 2048
NH, HN = 16, 128
NT = 2048
SL = S // NCORES
NG = 64
FI = 5632
EPS_LN = 1e-5
EPS_RMS = 1e-6
RSQ = 1.0 / math.sqrt(128.0)

KOFF = 0
VOFF = KOFF + NG * 128
WOFF = VOFF + NG * 128
DOFF = WOFF + 11 * 4 * 128
NOFF = DOFF + 2 * 11 * 128
WSOFF = NOFF + 1
RT = WSOFF + 13

DEBUG_TAPS = []
import os as _os
KSTAGE = int(_os.environ.get("KSTAGE", "9"))
KSUB = int(_os.environ.get("KSUB", "9"))
KOWNWAITS = int(_os.environ.get("KOWNWAITS", "0"))

ENG_LIST = ["sync", "gpsimd", "vector", "scalar", "tensor"]
DMA_SEM = {"sync": "dS", "gpsimd": "dG", "scalar": "dA"}
CMP_SEM = {"gpsimd": "G", "vector": "V", "scalar": "S", "tensor": "T"}
DMA_STREAM_SEMS = (["dX", "dXb", "dPK", "dPKb", "dKV", "dKVb",
                     "dWB", "dWBb", "dWS", "dWSb", "dW", "dWb",
                     "dS2", "dMK", "dY", "dYb", "dY2", "dY2b"])
ALL_SEMS = ["T", "V", "S", "G", "dS", "dG", "dA", "CC"] + DMA_STREAM_SEMS


class KCtx:

    def __init__(self, nc):
        self.nc = nc
        self.items = []
        self.cnt = {s: 0 for s in ALL_SEMS}

    enabled = True

    def _emit(self, eng, builder, waits, sig_sem, amt):
        if not self.enabled:
            return (sig_sem, 0) if sig_sem is not None else None
        flat = []
        for w in waits:
            if w is None:
                continue
            flat.append(w)
        ev = None
        if sig_sem is not None:
            self.cnt[sig_sem] += amt
            ev = (sig_sem, self.cnt[sig_sem])
        self.items.append((eng, builder, tuple(flat), sig_sem, amt))
        return ev

    def t(self, builder, waits=(), sig=False):
        return self._emit("tensor", builder, waits, "T" if sig else None, 1)

    def v(self, builder, waits=(), sig=False):
        return self._emit("vector", builder, waits, "V" if sig else None, 1)

    def s(self, builder, waits=(), sig=False):
        return self._emit("scalar", builder, waits, "S" if sig else None, 1)

    def g(self, builder, waits=(), sig=False):
        return self._emit("gpsimd", builder, waits, "G" if sig else None, 1)

    def dma(self, q, out, in_, waits=(), sem=None):
        return self._emit(
            q, (lambda e, o=out, i=in_: e.dma_start(out=o, in_=i)),
            waits, sem if sem is not None else DMA_SEM[q], 16)

    def all_dma_evs(self):
        return tuple((s, self.cnt[s]) for s in
                     ["dS", "dG", "dA"] + DMA_STREAM_SEMS if self.cnt[s])

    def coll(self, builder, waits=()):
        return self._emit("gpsimd", builder, waits, "CC", 1)

    def wait_only(self, eng, waits):
        self._emit(eng, lambda e: e.engine_nop() if eng != "sync"
                   else e.nop(), waits, None, 0)

    def dma_ev(self, q):
        sem = DMA_SEM.get(q, q)
        return (sem, self.cnt[sem])

    def build(self, block, sems):
        def section(eng_name):
            own_sem = CMP_SEM.get(eng_name)

            def body(eng):
                hw = {s: 0 for s in ALL_SEMS}
                own_count = 0
                for (en, builder, waits, sig_sem, amt) in self.items:
                    if en != eng_name:
                        continue
                    for (ws, wv) in waits:
                        if ws == own_sem:
                            if wv <= own_count:
                                if KOWNWAITS:
                                    eng.wait_ge(sems[ws], wv)
                                continue
                            raise RuntimeError(
                                f"{eng_name} waits own future {ws}>={wv}")
                        if hw[ws] >= wv:
                            continue
                        eng.wait_ge(sems[ws], wv)
                        hw[ws] = wv
                    inst = builder(eng)
                    if sig_sem is not None:
                        inst.then_inc(sems[sig_sem], amt)
                        if sig_sem == own_sem:
                            own_count += amt
            return body

        for name in ENG_LIST:
            getattr(block, name)(section(name))


def stage_inputs(inp):
    x = np.ascontiguousarray(np.asarray(inp["L0_attn_query"]))
    Wqkv = np.asarray(inp["L0_attn_Wqkv"])
    Wproj = np.asarray(inp["L0_attn_Wproj"])
    bproj = np.asarray(inp["L0_attn_bproj"])
    gup = np.asarray(inp["L0_ffn_gate_up_packed"])
    gus = np.asarray(inp["L0_ffn_gate_up_scales"])
    dwp = np.asarray(inp["L0_ffn_down_packed"])
    dws = np.asarray(inp["L0_ffn_down_scales"])

    WqkvT = np.ascontiguousarray(Wqkv.T)
    WprojT = np.ascontiguousarray(Wproj.T)
    bproj2 = np.ascontiguousarray(bproj.reshape(16, 128).T)
    gusT = np.ascontiguousarray(gus.T)
    dwsT = np.ascontiguousarray(dws.T)

    in_maps = []
    for c in range(NCORES):
        sl = slice(SL * c, SL * (c + 1))
        xT = np.ascontiguousarray(
            x[:, sl, :].transpose(2, 0, 1).reshape(H, NT))
        rows = np.concatenate(
            [np.arange(128 * j, 128 * (j + 1)) for j in range(c, 88, 8)])
        in_maps.append({
            "xT": xT,
            "WqkvT": WqkvT,
            "WprojT": WprojT,
            "bproj": bproj2,
            "gup": np.ascontiguousarray(gup[rows, :]),
            "gus": np.ascontiguousarray(gusT[rows, :]),
            "dwp": np.ascontiguousarray(dwp[256 * c:256 * (c + 1), :]),
            "dws": np.ascontiguousarray(dwsT[256 * c:256 * (c + 1), :]),
        })
    return in_maps


def assemble_output(results):
    out = np.empty((B, S, H), dtype=np.float32)
    for c in range(NCORES):
        yT = np.asarray(results[c]["OUT"])
        out[:, SL * c:SL * (c + 1), :] = \
            yT.reshape(H, B, SL).transpose(1, 2, 0)
    return out


def build_program():
    nc = bass.Bass("TRN2", target_bir_lowering=False, debug=False)
    K = KCtx(nc)

    ap_xT = nc.dram_tensor("xT", [H, NT], F32, kind="ExternalInput").ap()
    ap_Wq = nc.dram_tensor("WqkvT", [H, 3 * H], F32, kind="ExternalInput").ap()
    ap_Wp = nc.dram_tensor("WprojT", [H, H], F32, kind="ExternalInput").ap()
    ap_bp = nc.dram_tensor("bproj", [128, 16], F32, kind="ExternalInput").ap()
    ap_gup = nc.dram_tensor("gup", [1408, 256], I32, kind="ExternalInput").ap()
    ap_gus = nc.dram_tensor("gus", [1408, 16], F16, kind="ExternalInput").ap()
    ap_dwp = nc.dram_tensor("dwp", [256, 704], I32, kind="ExternalInput").ap()
    ap_dws = nc.dram_tensor("dws", [256, 44], F16, kind="ExternalInput").ap()
    ap_out = nc.dram_tensor("OUT", [H, NT], F32, kind="ExternalOutput").ap()
    dbg = {}
    for name, shape, dt in DEBUG_TAPS:
        dbg[name] = nc.dram_tensor(name, shape, dt, kind="ExternalOutput").ap()

    agin = nc.dram_tensor("agin", [RT, 512], BF16).ap()
    agout = nc.dram_tensor("agout", [NCORES * RT, 512], BF16).ap()
    qscr = nc.dram_tensor("qscr", [H, NT], BF16).ap()
    x2scr = nc.dram_tensor("x2scr", [H, NT], F32).ap()
    yscr = nc.dram_tensor("yscr", [H, NT], F32).ap()

    st = ExitStack()
    with st:
        sems = {s: st.enter_context(nc.semaphore("sem_" + s))
                for s in ALL_SEMS}

        blkA_scope = ExitStack()
        blkA = [blkA_scope.enter_context(
            nc.sbuf_tensor(f"A{i}", [128, NT], BF16, side="right"))
            for i in range(16)]
        ones128b = st.enter_context(nc.sbuf_tensor("ones128b", [128, 1], BF16))
        ones1b = st.enter_context(nc.sbuf_tensor("ones1b", [1, 128], BF16))
        ones1f = st.enter_context(nc.sbuf_tensor("ones1f", [1, 128], F32))
        ident = st.enter_context(nc.sbuf_tensor("ident", [128, 128], BF16))
        r_mq = st.enter_context(nc.sbuf_tensor("r_mq", [1, 64], F32))
        r_mkl = st.enter_context(nc.sbuf_tensor("r_mkl", [1, 64], F32))
        r_mkb = st.enter_context(nc.sbuf_tensor("r_mkb", [1, 64], BF16))
        r_a3f = st.enter_context(nc.sbuf_tensor("r_a3f", [1, NT], F32))
        r_nm3b = st.enter_context(nc.sbuf_tensor("r_nm3b", [1, NT], BF16))
        cst_eps = st.enter_context(nc.sbuf_tensor("cst_eps", [1, 1], F32))
        cst_eps2 = st.enter_context(nc.sbuf_tensor("cst_eps2", [1, 1], F32))

        K.g(lambda e: e.memset(ones128b[:, :], 1.0))
        K.g(lambda e: e.memset(cst_eps[:, :], EPS_LN))
        K.g(lambda e: e.memset(cst_eps2[:, :], EPS_RMS * EPS_LN))
        K.g(lambda e: e.memset(ones1b[:, :], 1.0))
        K.g(lambda e: e.memset(ones1f[:, :], 1.0))
        ev_memset = K.g(lambda e: e.memset(ident[:, :], 0.0), sig=True)
        ev_ident = K.g(lambda e: e.affine_select(
            out=ident[:, :], in_=ident[:, :], compare_op=ALU.not_equal,
            fill=1.0, base=0, pattern=[[-1, 128]], channel_multiplier=1),
            waits=(ev_memset,), sig=True)

        p1_scope = ExitStack()
        with p1_scope as p1:
            r_a1 = p1.enter_context(nc.sbuf_tensor("r_a1", [1, NT], F32))
            r_nm1 = p1.enter_context(nc.sbuf_tensor("r_nm1", [1, NT], F32))
            r_nm1b = p1.enter_context(nc.sbuf_tensor("r_nm1b", [1, NT], BF16))
            r_t1 = p1.enter_context(nc.sbuf_tensor("r_t1", [1, 512], F32))
            r_t2 = p1.enter_context(nc.sbuf_tensor("r_t2", [1, 512], F32))
            a1bc = [p1.enter_context(
                nc.sbuf_tensor(f"a1bc{i}", [128, 512], F32))
                for i in range(4)]

            _sc1 = ExitStack()
            with _sc1 as _s1:
                ps_st = _s1.enter_context(nc.psum_tensor("ps_st", [33, 512], F32))
                ps_bc = _s1.enter_context(nc.psum_tensor("ps_bc", [128, 512], F32))
                xf = _s1.enter_context(nc.sbuf_tensor("xf", [128, 16 * 512], F32))
                sqb = _s1.enter_context(nc.sbuf_tensor("sqb", [128, 512], BF16))
                r_a1b = _s1.enter_context(nc.sbuf_tensor("r_a1b", [1, NT], BF16))
                xf_v = xf.rearrange("p (t n) -> p t n", t=16)
                xT_v = ap_xT.rearrange("(t p) n -> p t n", p=128)
                ev_bc_prev = None
                ev_xbf_last = None
                ev_sq = None
                ev_st_rd = None
                for tc in range(4):
                    tsl = slice(512 * tc, 512 * (tc + 1))
                    ev_ld = K.dma("sync", xf_v[:, :, :], xT_v[:, :, tsl],
                                  waits=(ev_xbf_last, ev_sq), sem="dX")
                    ev_sq_mm = None
                    for ht in range(16):
                        ev_xbf_last = K.v(
                            lambda e, ht=ht, tsl=tsl: e.tensor_copy(
                                out=blkA[ht][:, tsl], in_=xf_v[:, ht, :]),
                            waits=(ev_ld,), sig=True)
                        ev_sq = K.s(lambda e, ht=ht: e.activation(
                            out=sqb[:, :], in_=xf_v[:, ht, :], func=AF.Square),
                            waits=(ev_ld, ev_sq_mm), sig=True)
                        K.t(lambda e, ht=ht, tsl=tsl: e.matmul(
                            ps_st[0:1, :], ones128b[:, 0:1], blkA[ht][:, tsl],
                            start=(ht == 0), stop=(ht == 15)),
                            waits=(ev_xbf_last, ev_st_rd, ev_ident))
                        ev_sq_mm = K.t(lambda e, ht=ht: e.matmul(
                            ps_st[32:33, :], ones128b[:, 0:1], sqb[:, :],
                            start=(ht == 0), stop=(ht == 15)),
                            waits=(ev_sq,), sig=True)
                        if ht == 15:
                            ev_sq_mm = K.t(lambda e: e.drain(), sig=True)
                    ev_mu = K.v(lambda e: e.tensor_scalar(
                        out=r_t1[0:1, :], in0=ps_st[0:1, :],
                        scalar1=1.0 / H, scalar2=None, op0=ALU.mult),
                        waits=(ev_sq_mm,), sig=True)
                    ev_nm = K.v(lambda e, tsl=tsl: e.tensor_scalar(
                        out=r_nm1[0:1, tsl], in0=ps_st[0:1, :],
                        scalar1=-1.0 / H, scalar2=None, op0=ALU.mult),
                        waits=(ev_sq_mm,), sig=True)
                    ev_muS = K.s(lambda e: e.nop(), waits=(ev_mu,),
                                 sig=True)
                    ev_m2 = K.v(lambda e: e.scalar_tensor_tensor(
                        out=r_t2[0:1, :], in0=r_t1[0:1, :], scalar=-1.0,
                        in1=r_t1[0:1, :], op0=ALU.mult, op1=ALU.mult),
                        waits=(ev_muS,), sig=True)
                    ev_st_rd = K.v(lambda e: e.scalar_tensor_tensor(
                        out=r_t1[0:1, :], in0=ps_st[32:33, :], scalar=1.0 / H,
                        in1=r_t2[0:1, :], op0=ALU.mult, op1=ALU.add),
                        waits=(ev_m2,), sig=True)
                    ev_sd = K.s(lambda e: e.activation(
                        out=r_t2[0:1, :], in_=r_t1[0:1, :], func=AF.Ln,
                        bias=cst_eps[0:1, 0:1], scale=1.0),
                        waits=(ev_st_rd, ev_ident), sig=True)
                    ev_sdV = K.v(lambda e: e.engine_nop(),
                                 waits=(ev_sd,), sig=True)
                    ev_a1 = K.s(lambda e, tsl=tsl: e.activation(
                        out=r_a1[0:1, tsl], in_=r_t2[0:1, :], func=AF.Exp,
                        scale=-0.5), waits=(ev_sdV,), sig=True)
                    ev_nmS = K.s(lambda e: e.nop(), waits=(ev_nm,),
                                 sig=True)
                    ev_nm1b = K.v(lambda e, tsl=tsl: e.tensor_copy(
                        out=r_nm1b[0:1, tsl], in_=r_nm1[0:1, tsl]),
                        waits=(ev_nmS,), sig=True)
                    K.t(lambda e, tsl=tsl: e.matmul(
                        ps_bc[:, :], ones1f[0:1, :], r_a1[0:1, tsl],
                        start=True, stop=True),
                        waits=(ev_a1, ev_bc_prev))
                    ev_bcm = K.t(lambda e: e.drain(), sig=True)
                    ev_bc_prev = K.s(lambda e, tc=tc: e.copy(
                        out=a1bc[tc][:, :], in_=ps_bc[:, :]),
                        waits=(ev_bcm,), sig=True)
                ev_rows1 = ev_bc_prev
                ev_nm1b_done = ev_nm1b

            K.enabled = KSTAGE >= 2
            if "d_xbf" in dbg:
                for htd in range(16):
                    K.dma("gpsimd", dbg["d_xbf"][128 * htd:128 * (htd + 1), :],
                          blkA[htd][:, :], waits=(ev_xbf_last, ev_rows1))
            if "d_rows" in dbg:
                K.dma("gpsimd", dbg["d_rows"][0:1, :], r_a1[0:1, :],
                      waits=(ev_rows1,))
                K.dma("gpsimd", dbg["d_rows"][1:2, :], r_nm1[0:1, :],
                      waits=(ev_rows1,))
            if "d_a1bc" in dbg:
                for tcd in range(4):
                    K.dma("gpsimd",
                          dbg["d_a1bc"][:, 512 * tcd:512 * (tcd + 1)],
                          a1bc[tcd][:, :], waits=(ev_rows1,))
            ev_mq_done = None
            _sc2 = ExitStack()
            with _sc2 as _s2:
                wst0 = _s2.enter_context(nc.sbuf_tensor("wst0", [128, 16 * 128], BF16))
                wst1 = _s2.enter_context(nc.sbuf_tensor("wst1", [128, 16 * 128], BF16))
                ods0 = _s2.enter_context(nc.sbuf_tensor("ods0", [128, NT], BF16))
                ods1 = _s2.enter_context(nc.sbuf_tensor("ods1", [128, NT], BF16))
                vts0 = _s2.enter_context(nc.sbuf_tensor("vts0", [128, NT], BF16))
                vts1 = _s2.enter_context(nc.sbuf_tensor("vts1", [128, NT], BF16))
                sqb2 = _s2.enter_context(nc.sbuf_tensor("sqb2", [128, NT], BF16))
                r_ws = _s2.enter_context(nc.sbuf_tensor("r_ws", [1, 128], BF16))
                r_nrm = _s2.enter_context(nc.sbuf_tensor("r_nrm", [1, NT], F32))
                pk0 = _s2.enter_context(nc.sbuf_tensor("pk0", [128, 704], I32))
                pk1 = _s2.enter_context(nc.sbuf_tensor("pk1", [128, 704], I32))
                dq_ni = _s2.enter_context(nc.sbuf_tensor("dq_ni", [128, 704], I32))
                dq_nf = _s2.enter_context(nc.sbuf_tensor("dq_nf", [128, 704], F32))
                dq_ind = _s2.enter_context(nc.sbuf_tensor("dq_ind", [128, 704], F32))
                dq_m = _s2.enter_context(nc.sbuf_tensor("dq_m", [128, 704], F32))
                dq_ta = _s2.enter_context(nc.sbuf_tensor("dq_ta", [128, 704], F32))
                dq_tb = _s2.enter_context(nc.sbuf_tensor("dq_tb", [128, 704], F32))
                wdeq = _s2.enter_context(nc.sbuf_tensor("wdeq", [128, FI], BF16))
                scl = _s2.enter_context(nc.sbuf_tensor("scl", [128, 44], F32))
                wTst = _s2.enter_context(nc.sbuf_tensor("wTst", [128, 512], BF16))
                r_wsum = _s2.enter_context(nc.sbuf_tensor("r_wsum", [1, 128], BF16))
                wsc = _s2.enter_context(nc.sbuf_tensor("wsc", [128, 1], F32))
                wscb = _s2.enter_context(nc.sbuf_tensor("wscb", [128, 1], BF16))
                ps_q0 = _s2.enter_context(nc.psum_tensor("ps_q0", [128, 512], F32))
                ps_q1 = _s2.enter_context(nc.psum_tensor("ps_q1", [128, 512], F32))
                ps_q2 = _s2.enter_context(nc.psum_tensor("ps_q2", [128, 512], F32))
                ps_q3 = _s2.enter_context(nc.psum_tensor("ps_q3", [128, 512], F32))
                ps_x1 = _s2.enter_context(nc.psum_tensor("ps_x1", [1, 512], F32))
                ps_x2 = _s2.enter_context(nc.psum_tensor("ps_x2", [1, 512], F32))
                ps_t0 = _s2.enter_context(nc.psum_tensor("ps_t0", [128, 128], BF16))
                ps_t1 = _s2.enter_context(nc.psum_tensor("ps_t1", [128, 128], BF16))
                p1a_gate = [ev_rows1, ev_nm1b_done, ev_sq_mm]
                ps_q = [ps_q0, ps_q1, ps_q2, ps_q3]
                wbufs = [wst0, wst1]
                odss = [ods0, ods1]
                vtss = [vts0, vts1]
                pks = [pk0, pk1]
                tr_ps = [ps_t0, ps_t1]
                tr_free = [ev_ident, ev_ident]
                w_ev = [None, None]
                w_free = [p1a_gate, p1a_gate]
                pk_ev = [None, None]
                pk_free = [p1a_gate, p1a_gate]
                evac_ev = [None, None, None, None]
                aux_ev = [None, None]
                ods_free = [[], []]
                vts_free = [[], []]
                ev_nrm_mm_last = None
                ev_dq_tr_last = None
                ev_wT_dma = None
                ev_wsum_dma = None
                ev_scl_free = None
                ev_mk_done = None
                wq_v = ap_Wq.rearrange("(t p) n -> p t n", p=128)

                dq_units = [("gu", i) for i in range(11)] + \
                           [("dn", i) for i in range(2)]
                dq_at = {}
                for i, u in enumerate(dq_units):
                    dq_at[2 + 3 * i] = (u, i)
                pk_ev[0] = K.dma("sync", pk0[:, 0:256], ap_gup[0:128, :],
                                 waits=tuple(p1a_gate), sem="dPK")

                def emit_dequant(unit, uidx):
                    nonlocal ev_scl_free, ev_dq_tr_last, ev_wT_dma, \
                        ev_wsum_dma
                    kind, idx = unit
                    C = 256 if kind == "gu" else 704
                    W = C * 8
                    ngrp = W // 128
                    buf = uidx % 2
                    pkb = pks[buf]
                    scl_w = ((ev_scl_free,) if ev_scl_free is not None
                             else tuple(p1a_gate))
                    if kind == "gu":
                        ev_s = K.dma("gpsimd", scl[:, 0:16],
                                     ap_gus[128 * idx:128 * (idx + 1), :],
                                     waits=scl_w, sem="dS2")
                    else:
                        ev_s = K.dma("gpsimd", scl[:, 0:44],
                                     ap_dws[128 * idx:128 * (idx + 1), :],
                                     waits=scl_w, sem="dS2")
                    wd_v = wdeq.rearrange("p (c k) -> p c k", k=8)
                    last_v = None
                    for kk in range(8):
                        K.v(lambda e, kk=kk, C=C: e.tensor_scalar(
                            out=dq_ni[:, 0:C], in0=pkb[:, 0:C],
                            scalar1=4 * kk, scalar2=15,
                            op0=ALU.logical_shift_right, op1=ALU.bitwise_and),
                            waits=((pk_ev[buf], ev_dq_tr_last)
                                   if kk == 0 else ()))
                        K.v(lambda e, C=C: e.tensor_copy(
                            out=dq_nf[:, 0:C], in_=dq_ni[:, 0:C]))
                        K.v(lambda e, C=C: e.tensor_scalar(
                            out=dq_ind[:, 0:C], in0=dq_nf[:, 0:C],
                            scalar1=7.5, scalar2=None, op0=ALU.is_ge))
                        K.v(lambda e, C=C: e.scalar_tensor_tensor(
                            out=dq_m[:, 0:C], in0=dq_ind[:, 0:C], scalar=-8.0,
                            in1=dq_nf[:, 0:C], op0=ALU.mult, op1=ALU.add))
                        K.v(lambda e, C=C: e.tensor_scalar(
                            out=dq_ta[:, 0:C], in0=dq_m[:, 0:C],
                            scalar1=0.5, scalar2=None, op0=ALU.mult))
                        K.v(lambda e, C=C: e.tensor_scalar(
                            out=dq_tb[:, 0:C], in0=dq_m[:, 0:C],
                            scalar1=2.0, scalar2=None, op0=ALU.subtract))
                        K.v(lambda e, C=C: e.tensor_tensor(
                            out=dq_ta[:, 0:C], in0=dq_ta[:, 0:C],
                            in1=dq_tb[:, 0:C], op=ALU.max))
                        K.v(lambda e, C=C: e.tensor_scalar(
                            out=dq_tb[:, 0:C], in0=dq_m[:, 0:C],
                            scalar1=2.0, scalar2=8.0,
                            op0=ALU.mult, op1=ALU.subtract))
                        K.v(lambda e, C=C: e.tensor_tensor(
                            out=dq_ta[:, 0:C], in0=dq_ta[:, 0:C],
                            in1=dq_tb[:, 0:C], op=ALU.max))
                        K.v(lambda e, C=C: e.tensor_scalar(
                            out=dq_ind[:, 0:C], in0=dq_ind[:, 0:C],
                            scalar1=-2.0, scalar2=1.0,
                            op0=ALU.mult, op1=ALU.add))
                        last_v = K.v(lambda e, kk=kk, C=C: e.tensor_tensor(
                            out=wd_v[:, 0:C, kk], in0=dq_ta[:, 0:C],
                            in1=dq_ind[:, 0:C], op=ALU.mult),
                            sig=(kk == 7))
                    pk_free[buf] = last_v
                    if uidx + 1 < len(dq_units):
                        nk, ni = dq_units[uidx + 1]
                        nbuf = (uidx + 1) % 2
                        src = (ap_gup[128 * ni:128 * (ni + 1), :]
                               if nk == "gu"
                               else ap_dwp[128 * ni:128 * (ni + 1), :])
                        ncols = 256 if nk == "gu" else 704
                        pfb = pk_free[nbuf]
                        pk_ev[nbuf] = K.dma(
                            "sync", pks[nbuf][:, 0:ncols], src,
                            waits=(tuple(pfb) if isinstance(pfb, list)
                                   else (pfb,)),
                            sem="dPK" if nbuf == 0 else "dPKb")
                    ev_scaled = None
                    for gi in range(ngrp):
                        ev_scaled = K.v(lambda e, gi=gi: e.tensor_scalar(
                            out=wdeq[:, 128 * gi:128 * (gi + 1)],
                            in0=wdeq[:, 128 * gi:128 * (gi + 1)],
                            scalar1=scl[:, gi:gi + 1], scalar2=None,
                            op0=ALU.mult),
                            waits=(last_v, ev_s) if gi == 0 else (),
                            sig=(gi == ngrp - 1))
                    ev_scl_free = ev_scaled
                    if uidx == 5 and "d_wdeq" in dbg:
                        ev_scl_free = K.dma("gpsimd", dbg["d_wdeq"],
                                            wdeq[:, :], waits=(ev_scaled,))
                        K.dma("gpsimd", dbg["d_scl"], scl[:, :],
                              waits=(ev_scaled,))
                    ev_red = K.v(lambda e, W=W: e.tensor_reduce(
                        out=wsc[:, 0:1], in_=wdeq[:, 0:W], axis=AX.X,
                        op=ALU.add), waits=(ev_scaled,), sig=True)
                    ev_redS = K.s(lambda e: e.nop(),
                                  waits=(ev_red,), sig=True)
                    ev_redb = K.v(lambda e: e.tensor_copy(
                        out=wscb[:, 0:1], in_=wsc[:, 0:1]),
                        waits=(ev_redS, ev_wsum_dma), sig=True)
                    row = (WSOFF + idx) if kind == "gu" else (WSOFF + 11 + idx)
                    ev_wsum_dma = K.dma("scalar", agin[row:row + 1, 0:128],
                                        wscb[:, 0:1], waits=(ev_redb,))
                    ntile = W // 128
                    for it in range(ntile):
                        pp = tr_ps[it % 2]
                        K.t(lambda e, it=it, pp=pp: e.transpose(
                            pp[:, :], wdeq[:, 128 * it:128 * (it + 1)],
                            ident[:, :]),
                            waits=(ev_scaled, tr_free[it % 2]))
                        ev_tr = K.t(lambda e: e.drain(), sig=True)
                        ev_ev = K.v(lambda e, it=it, pp=pp: e.tensor_copy(
                            out=wTst[:, 128 * (it % 4):128 * (it % 4 + 1)],
                            in_=pp[:, :]),
                            waits=(ev_tr, ev_wT_dma), sig=True)
                        tr_free[it % 2] = ev_ev
                        ev_dq_tr_last = ev_tr
                        if it % 4 == 3:
                            if kind == "gu":
                                rowb = WOFF + (idx * 4 + it // 4) * 128
                            else:
                                rowb = DOFF + (idx * 11 + it // 4) * 128
                            ev_wT_dma = K.dma(
                                "scalar", agin[rowb:rowb + 128, :],
                                wTst[:, :], waits=(ev_ev,))

                if "d_w0" in dbg:
                    pass
                for ot in range(48):
                    buf = ot % 2
                    head, part = ot // 3, ot % 3
                    wv = wbufs[buf].rearrange("p (t n) -> p t n", t=16)
                    wfb = w_free[buf]
                    w_ev[buf] = K.dma(
                        "gpsimd", wv[:, :, :],
                        wq_v[:, :, 128 * ot:128 * (ot + 1)],
                        waits=(tuple(wfb) if isinstance(wfb, list) else (wfb,)),
                        sem="dW" if buf == 0 else "dWb")
                    for ht in range(16):
                        K.t(lambda e, ht=ht, wv=wv: e.matmul(
                            ps_x1[0:1, 0:128], ones128b[:, 0:1], wv[:, ht, :],
                            start=(ht == 0), stop=(ht == 15)),
                            waits=((w_ev[buf], aux_ev[0]) if ht == 0 else ()))
                    ev_ws_mm = K.t(lambda e: e.drain(), sig=True)
                    ev_ws = K.v(lambda e: e.tensor_copy(
                        out=r_ws[0:1, :], in_=ps_x1[0:1, 0:128]),
                        waits=(ev_ws_mm,), sig=True)
                    aux_ev[0] = ev_ws
                    if ot == 0 and "d_rws0" in dbg:
                        K.dma("gpsimd", dbg["d_rws0"], r_ws[0:1, :],
                              waits=(ev_ws,))
                        if "d_ps0" in dbg:
                            ev_rws_v2 = K.v(lambda e: e.tensor_copy(
                                out=r_mkb[0:1, 0:64], in_=ps_x1[0:1, 0:64]),
                                waits=(ev_ws,), sig=True)
                            K.dma("gpsimd", dbg["d_ps0"],
                                  r_mkb[0:1, 0:64], waits=(ev_rws_v2,))
                    if ot == 0 and "d_w0" in dbg:
                        K.dma("gpsimd", dbg["d_w0"], wbufs[0][:, :],
                              waits=(w_ev[0],))
                    ods = odss[buf]
                    for tc in range(4):
                        tsl = slice(512 * tc, 512 * (tc + 1))
                        bank = tc
                        for ht in range(16):
                            K.t(lambda e, b=bank, ht=ht, wv=wv, tsl=tsl:
                                e.matmul(
                                    ps_q[b][:, :], wv[:, ht, :],
                                    blkA[ht][:, tsl],
                                    start=(ht == 0), stop=False),
                                waits=((w_ev[buf], evac_ev[bank], ev_rows1)
                                       if ht == 0 else ()))
                        K.t(lambda e, b=bank, tsl=tsl: e.matmul(
                            ps_q[b][:, :], r_ws[0:1, :], r_nm1b[0:1, tsl],
                            start=False, stop=True),
                            waits=(ev_ws, ev_nm1b_done))
                        mm_ev = K.t(lambda e: e.drain(), sig=True)
                        evac_ev[bank] = K.v(
                            lambda e, b=bank, tsl=tsl, ods=ods, tc=tc:
                            e.tensor_tensor(
                                out=ods[:, tsl], in0=ps_q[b][:, :],
                                in1=a1bc[tc][:, :], op=ALU.mult),
                            waits=(mm_ev,) + tuple(ods_free[buf]), sig=True)
                        ods_free[buf] = []
                    ev_od_all = tuple(evac_ev)
                    w_free[buf] = evac_ev[3]
                    if ot == 0 and "d_ods0" in dbg:
                        K.dma("gpsimd", dbg["d_ods0"], ods[:, :],
                              waits=ev_od_all)
                    if ot == 3 and "d_ods3" in dbg:
                        K.dma("gpsimd", dbg["d_ods3"], ods[:, :],
                              waits=ev_od_all)
                    if ot == 3 and "d_rws3" in dbg:
                        K.dma("gpsimd", dbg["d_rws3"], r_ws[0:1, :],
                              waits=(ev_ws,))

                    if part == 0:
                        K.dma("scalar", qscr[128 * head:128 * (head + 1), :],
                              ods[:, :], waits=ev_od_all)
                    elif part == 1:
                        for bb in range(4):
                            g = bb * 16 + head
                            K.dma("scalar",
                                  agin[KOFF + 128 * g:KOFF + 128 * (g + 1),
                                       :],
                                  ods[:, 512 * bb:512 * (bb + 1)],
                                  waits=ev_od_all if bb == 0 else ())

                    if part in (0, 1):
                        ev_sq2 = K.s(lambda e, ods=ods: e.activation(
                            out=sqb2[:, :], in_=ods[:, :], func=AF.Square),
                            waits=ev_od_all + (ev_nrm_mm_last,), sig=True)
                        ev_nc = None
                        for bb in range(4):
                            pa = [ps_x1, ps_x2][bb % 2]
                            K.t(lambda e, bb=bb, pa=pa: e.matmul(
                                pa[0:1, :], ones128b[:, 0:1],
                                sqb2[:, 512 * bb:512 * (bb + 1)],
                                start=True, stop=True),
                                waits=(ev_sq2, aux_ev[bb % 2]))
                            ev_nmm = K.t(lambda e: e.drain(), sig=True)
                            ev_nc = K.v(lambda e, bb=bb, pa=pa: e.tensor_copy(
                                out=r_nrm[0:1, 512 * bb:512 * (bb + 1)],
                                in_=pa[0:1, :]), waits=(ev_nmm,), sig=True)
                            aux_ev[bb % 2] = ev_nc
                            ev_nrm_mm_last = ev_nmm
                        tgt = r_mq if part == 0 else r_mkl
                        ev_mx = None
                        for bb in range(4):
                            g = bb * 16 + head
                            ev_mx = K.v(lambda e, g=g, bb=bb, tgt=tgt:
                                        e.reduce_max(
                                out=tgt[0:1, g:g + 1],
                                in_=r_nrm[0:1, 512 * bb:512 * (bb + 1)],
                                axis=AX.X), waits=(ev_nc,), sig=(bb == 3))
                        ods_free[buf] = [ev_sq2, K.dma_ev("scalar")]
                        if part == 0:
                            ev_mq_done = ev_mx
                        else:
                            ev_mk_done = ev_mx
                    else:
                        vts = vtss[(ot // 3) % 2]
                        ev_ev_all = []
                        for s16 in range(16):
                            pp = tr_ps[s16 % 2]
                            K.t(lambda e, s16=s16, pp=pp, ods=ods:
                                e.transpose(
                                pp[:, :], ods[:, 128 * s16:128 * (s16 + 1)],
                                ident[:, :]),
                                waits=(tr_free[s16 % 2],) + (
                                    ev_od_all if s16 == 0 else ()))
                            ev_tr = K.t(lambda e: e.drain(), sig=True)
                            ev_ev = K.v(lambda e, s16=s16, pp=pp, vts=vts:
                                        e.tensor_copy(
                                out=vts[:, 128 * s16:128 * (s16 + 1)],
                                in_=pp[:, :]),
                                waits=(ev_tr,) + tuple(
                                    vts_free[(ot // 3) % 2]), sig=True)
                            vts_free[(ot // 3) % 2] = []
                            tr_free[s16 % 2] = ev_ev
                            ev_ev_all.append(ev_ev)
                        for bb in range(4):
                            g = bb * 16 + head
                            K.dma("scalar",
                                  agin[VOFF + 128 * g:VOFF + 128 * (g + 1),
                                       :],
                                  vts[:, 512 * bb:512 * (bb + 1)],
                                  waits=(ev_ev_all[4 * bb + 3],))
                        vts_free[(ot // 3) % 2] = [ev_ev_all[-1],
                                                   K.dma_ev("scalar")]
                        ods_free[buf] = [ev_ev_all[-1]]
                        if ot == 2 and "d_vts" in dbg:
                            K.dma("gpsimd", dbg["d_vts"], vts[:, :],
                                  waits=(ev_ev_all[-1],))
                        if ot == 2 and "d_ods2" in dbg:
                            K.dma("gpsimd", dbg["d_ods2"], ods[:, :],
                                  waits=ev_od_all)

                    if ot in dq_at:
                        emit_dequant(*dq_at[ot])

                ev_mkS = K.s(lambda e: e.nop(),
                             waits=(ev_mk_done,), sig=True)
                ev_mkb = K.v(lambda e: e.tensor_scalar(
                    out=r_mkb[0:1, :], in0=r_mkl[0:1, :],
                    scalar1=1.0157, scalar2=None, op0=ALU.mult),
                    waits=(ev_mkS,), sig=True)
                K.dma("scalar", agin[NOFF:NOFF + 1, 0:64],
                      r_mkb[0:1, :], waits=(ev_mkb,))

                p1_tail = (ev_mq_done, aux_ev[0], aux_ev[1],
                           tr_free[0], tr_free[1], evac_ev[3],
                           ev_nrm_mm_last)

            K.enabled = KSTAGE >= 3
            ev_cc = K.coll(lambda e: e.collective_compute(
                "AllGather", ALU.bypass,
                replica_groups=[list(range(NCORES))],
                ins=[agin], outs=[agout]),
                waits=K.all_dma_evs() + p1_tail)

        if "d_agin" in dbg:
            K.dma("gpsimd", dbg["d_agin"], agin, waits=(ev_cc,))
        if "d_qscr" in dbg:
            K.dma("gpsimd", dbg["d_qscr"], qscr, waits=(ev_cc,))

        K.enabled = KSTAGE >= 4
        _sc3 = ExitStack()
        with _sc3 as _s3:
            p_sb = _s3.enter_context(nc.sbuf_tensor("p_sb", [128, 32 * 512], BF16))
            k_sb0 = _s3.enter_context(nc.sbuf_tensor("k_sb0", [128, 4096], BF16))
            k_sb1 = _s3.enter_context(nc.sbuf_tensor("k_sb1", [128, 4096], BF16))
            v_sb0 = _s3.enter_context(nc.sbuf_tensor("v_sb0", [128, 4096], BF16))
            v_sb1 = _s3.enter_context(nc.sbuf_tensor("v_sb1", [128, 4096], BF16))
            q_sb0 = _s3.enter_context(nc.sbuf_tensor("q_sb0", [128, 512], BF16))
            q_sb1 = _s3.enter_context(nc.sbuf_tensor("q_sb1", [128, 512], BF16))
            mk8 = _s3.enter_context(nc.sbuf_tensor("mk8", [1, 512], BF16))
            r_mk2 = _s3.enter_context(nc.sbuf_tensor("r_mk2", [1, 64], F32))
            r_bneg = _s3.enter_context(nc.sbuf_tensor("r_bneg", [1, 64], BF16))
            r_z = _s3.enter_context(nc.sbuf_tensor("r_z", [1, 512], F32))
            r_zib = _s3.enter_context(nc.sbuf_tensor("r_zib", [1, 512], BF16))
            bias_sb = _s3.enter_context(nc.sbuf_tensor("bias_sb", [128, 1], F32))
            zbc_sb = _s3.enter_context(nc.sbuf_tensor("zbc_sb", [128, 512], F32))
            ps_sA = _s3.enter_context(nc.psum_tensor("ps_sA", [128, 1024], F32))
            ps_sB = _s3.enter_context(nc.psum_tensor("ps_sB", [128, 1024], F32))
            ps_ctx = _s3.enter_context(nc.psum_tensor("ps_ctx", [128, 512], F32))
            ps_z = _s3.enter_context(nc.psum_tensor("ps_z", [1, 512], F32))
            ps_bA = _s3.enter_context(nc.psum_tensor("ps_bA", [128, 512], F32))
            ps_bB = _s3.enter_context(nc.psum_tensor("ps_bB", [128, 512], F32))
            ev_mk_ld = None
            for rk in range(NCORES):
                ev_mk_ld = K.dma(
                    "sync", mk8[0:1, 64 * rk:64 * (rk + 1)],
                    agout[rk * RT + NOFF:rk * RT + NOFF + 1, 0:64],
                    waits=(ev_cc,), sem="dMK")
            ev_m = K.v(lambda e: e.tensor_tensor(
                out=r_mk2[0:1, :], in0=mk8[0:1, 0:64], in1=mk8[0:1, 64:128],
                op=ALU.max), waits=(ev_mk_ld,), sig=True)
            for rk in range(2, NCORES):
                ev_mS = K.s(lambda e: e.nop(), waits=(ev_m,), sig=True)
                ev_m = K.v(lambda e, rk=rk: e.tensor_tensor(
                    out=r_mk2[0:1, :], in0=r_mk2[0:1, :],
                    in1=mk8[0:1, 64 * rk:64 * (rk + 1)], op=ALU.max),
                    waits=(ev_mS,), sig=True)
            ev_mqS = K.s(lambda e: e.nop(),
                         waits=(ev_mq_done,), sig=True)
            ev_b1 = K.v(lambda e: e.tensor_tensor(
                out=r_mk2[0:1, :], in0=r_mk2[0:1, :], in1=r_mq[0:1, :],
                op=ALU.mult), waits=(ev_m, ev_mqS), sig=True)
            ev_b2 = K.s(lambda e: e.activation(
                out=r_mk2[0:1, :], in_=r_mk2[0:1, :], func=AF.Sqrt),
                waits=(ev_b1,), sig=True)
            ev_b3 = K.v(lambda e: e.tensor_scalar(
                out=r_bneg[0:1, :], in0=r_mk2[0:1, :], scalar1=-RSQ,
                scalar2=None, op0=ALU.mult), waits=(ev_b2,), sig=True)

            kbufs, vbufs, qbufs = [k_sb0, k_sb1], [v_sb0, v_sb1], \
                [q_sb0, q_sb1]
            k_ev = [None, None]
            v_ev = [None, None]
            q_ev = [None, None]
            k_free = [None, None]
            v_free = [None, None]
            q_free = [None, None]

            def load_g(g, buf):
                head, bb = g % 16, g // 16
                for rk in range(NCORES):
                    k_ev[buf] = K.dma(
                        "sync", kbufs[buf][:, 512 * rk:512 * (rk + 1)],
                        agout[rk * RT + KOFF + 128 * g:
                              rk * RT + KOFF + 128 * (g + 1), :],
                        waits=(ev_cc, k_free[buf]) if rk == 0 else (),
                        sem="dKV" if buf == 0 else "dKVb")
                    v_ev[buf] = K.dma(
                        "sync", vbufs[buf][:, 512 * rk:512 * (rk + 1)],
                        agout[rk * RT + VOFF + 128 * g:
                              rk * RT + VOFF + 128 * (g + 1), :],
                        waits=(v_free[buf],) if rk == 0 else (),
                        sem="dKV" if buf == 0 else "dKVb")
                q_ev[buf] = K.dma(
                    "sync", qbufs[buf][:, :],
                    qscr[128 * head:128 * (head + 1),
                         512 * bb:512 * (bb + 1)],
                    waits=(q_free[buf],), sem="dKV" if buf == 0 else "dKVb")

            load_g(0, 0)
            sgrp_free = [None, None]
            ev_bias = None
            ev_zbc = None
            ev_ctx_evac = None
            ev_z_rd = None
            last_ctx_ev = None
            for g in range(NG):
                buf = g % 2
                n_tile, bbs = g // 4, g % 4
                if g + 1 < NG:
                    load_g(g + 1, 1 - buf)
                ev_bm = K.t(lambda e, g=g: e.matmul(
                    ps_bA[:, 0:1], ones1b[0:1, :], r_bneg[0:1, g:g + 1],
                    start=True, stop=True),
                    waits=(ev_b3, ev_bias), sig=True)
                ev_bias = K.s(lambda e: e.copy(
                    out=bias_sb[:, 0:1], in_=ps_bA[:, 0:1]),
                    waits=(ev_bm,), sig=True)
                exp_ev = [None] * 16
                for r in range(16):
                    grp = r % 2
                    pg = [ps_sA, ps_sB][grp]
                    ev_smm = None
                    for j in range(2):
                        kt = 2 * r + j
                        ev_smm = K.t(lambda e, kt=kt, pg=pg, j=j, buf=buf:
                                     e.matmul(
                            pg[:, 512 * j:512 * (j + 1)],
                            kbufs[buf][:, 128 * kt:128 * (kt + 1)],
                            qbufs[buf][:, :], start=True, stop=True),
                            waits=(((k_ev[buf], q_ev[buf],
                                     sgrp_free[grp]) if r < 2 else
                                    (sgrp_free[grp],)) if j == 0 else ()),
                            sig=(j == 1))
                    exp_ev[r] = K.s(lambda e, r=r, pg=pg: e.activation(
                        out=p_sb[:, 1024 * r:1024 * (r + 1)], in_=pg[:, :],
                        func=AF.Exp, bias=bias_sb[:, 0:1], scale=RSQ),
                        waits=(ev_smm, ev_bias), sig=True)
                    sgrp_free[grp] = exp_ev[r]
                    if r > 0:
                        for j in range(2):
                            kt = 2 * (r - 1) + j
                            K.t(lambda e, kt=kt: e.matmul(
                                ps_z[0:1, :], ones128b[:, 0:1],
                                p_sb[:, 512 * kt:512 * (kt + 1)],
                                start=(kt == 0), stop=False),
                                waits=(exp_ev[r - 1], ev_z_rd)
                                if j == 0 else ())
                            K.t(lambda e, kt=kt, buf=buf: e.matmul(
                                ps_ctx[:, :],
                                vbufs[buf][:, 128 * kt:128 * (kt + 1)],
                                p_sb[:, 512 * kt:512 * (kt + 1)],
                                start=(kt == 0), stop=False),
                                waits=(v_ev[buf], ev_ctx_evac)
                                if kt == 0 else ())
                ev_zmm = None
                ev_pmm = None
                for j in range(2):
                    kt = 30 + j
                    ev_zmm = K.t(lambda e, kt=kt: e.matmul(
                        ps_z[0:1, :], ones128b[:, 0:1],
                        p_sb[:, 512 * kt:512 * (kt + 1)],
                        start=False, stop=(kt == 31)),
                        waits=(exp_ev[15],) if j == 0 else (),
                        sig=(kt == 31))
                    ev_pmm = K.t(lambda e, kt=kt, buf=buf: e.matmul(
                        ps_ctx[:, :],
                        vbufs[buf][:, 128 * kt:128 * (kt + 1)],
                        p_sb[:, 512 * kt:512 * (kt + 1)],
                        start=False, stop=(kt == 31)),
                        sig=(kt == 31))
                k_free[buf] = ev_zmm
                q_free[buf] = ev_zmm
                v_free[buf] = ev_pmm
                ev_z1 = K.v(lambda e: e.tensor_copy(
                    out=r_z[0:1, :], in_=ps_z[0:1, :]),
                    waits=(ev_zmm,), sig=True)
                ev_z_rd = ev_z1
                ev_z1S = K.s(lambda e: e.nop(), waits=(ev_z1,), sig=True)
                ev_z2 = K.v(lambda e: e.reciprocal(
                    out=r_z[0:1, :], in_=r_z[0:1, :]), waits=(ev_z1S,),
                    sig=True)
                ev_zm = K.t(lambda e: e.matmul(
                    ps_bB[:, :], ones1f[0:1, :], r_z[0:1, :],
                    start=True, stop=True),
                    waits=(ev_z2, ev_zbc), sig=True)
                ev_zbc = K.s(lambda e: e.copy(
                    out=zbc_sb[:, :], in_=ps_bB[:, :]),
                    waits=(ev_zm, ev_ctx_evac), sig=True)
                ev_ctx_evac = K.v(
                    lambda e, n_tile=n_tile, bbs=bbs: e.tensor_tensor(
                        out=blkA[n_tile][:, 512 * bbs:512 * (bbs + 1)],
                        in0=ps_ctx[:, :], in1=zbc_sb[:, :], op=ALU.mult),
                    waits=(ev_pmm, ev_zbc), sig=True)
                last_ctx_ev = ev_ctx_evac

        if "d_rhs" in dbg:
            for n in range(16):
                K.dma("gpsimd", dbg["d_rhs"][128 * n:128 * (n + 1), :],
                      blkA[n][:, :], waits=(last_ctx_ev,))

        K.enabled = KSTAGE >= 5
        _sc4 = ExitStack()
        with _sc4 as _s4:
            wp0 = _s4.enter_context(nc.sbuf_tensor("wp0", [128, 16 * 128], BF16))
            wp1 = _s4.enter_context(nc.sbuf_tensor("wp1", [128, 16 * 128], BF16))
            out1 = _s4.enter_context(nc.sbuf_tensor("out1", [128, 16 * 512], F32))
            o1b = _s4.enter_context(nc.sbuf_tensor("o1b", [128, 512], BF16))
            sqb3 = _s4.enter_context(nc.sbuf_tensor("sqb3", [128, 512], BF16))
            bp_sb = _s4.enter_context(nc.sbuf_tensor("bp_sb", [128, 16], F32))
            x1t0 = _s4.enter_context(nc.sbuf_tensor("x1t0", [128, 512], F32))
            x1t1 = _s4.enter_context(nc.sbuf_tensor("x1t1", [128, 512], F32))
            x2t0 = _s4.enter_context(nc.sbuf_tensor("x2t0", [128, 512], F32))
            x2t1 = _s4.enter_context(nc.sbuf_tensor("x2t1", [128, 512], F32))
            x2b = _s4.enter_context(nc.sbuf_tensor("x2b", [128, 512], BF16))
            r_p1 = _s4.enter_context(nc.sbuf_tensor("r_p1", [1, 512], F32))
            r_p2 = _s4.enter_context(nc.sbuf_tensor("r_p2", [1, 512], F32))
            r_p1b = _s4.enter_context(nc.sbuf_tensor("r_p1b", [1, 512], BF16))
            r_p2b = _s4.enter_context(nc.sbuf_tensor("r_p2b", [1, 512], BF16))
            abc_sb = _s4.enter_context(nc.sbuf_tensor("abc_sb", [128, 512], F32))
            nbc_sb = _s4.enter_context(nc.sbuf_tensor("nbc_sb", [128, 512], F32))
            ps_p0 = _s4.enter_context(nc.psum_tensor("ps_p0", [128, 512], F32))
            ps_p1 = _s4.enter_context(nc.psum_tensor("ps_p1", [128, 512], F32))
            ps_p2 = _s4.enter_context(nc.psum_tensor("ps_p2", [128, 512], F32))
            ps_p3 = _s4.enter_context(nc.psum_tensor("ps_p3", [128, 512], F32))
            ps_s2 = _s4.enter_context(nc.psum_tensor("ps_s2", [33, 512], F32))
            ps_s3 = _s4.enter_context(nc.psum_tensor("ps_s3", [33, 512], F32))
            ps_b3 = _s4.enter_context(nc.psum_tensor("ps_b3", [128, 1024], F32))
            ps_p = [ps_p0, ps_p1, ps_p2, ps_p3]
            gate3 = (last_ctx_ev,)
            ev_bp = K.dma("sync", bp_sb[:, :], ap_bp, waits=gate3, sem="dMK")
            wq_p = ap_Wp.rearrange("(t p) n -> p t n", p=128)
            wpb = [wp0, wp1]
            wp_ev = [None, None]
            wp_free = [None, None]
            pevac = [None, None, None, None]
            out1_v = out1.rearrange("p (t n) -> p t n", t=16)
            x2scr_v = x2scr.rearrange("(t p) n -> p t n", p=128)
            xT_v2 = ap_xT.rearrange("(t p) n -> p t n", p=128)
            out1_free = [None] * 16
            x1_free = [None, None]
            x2dma = [None, None]
            ev_sqmm = None
            ev_o1b_mm = None
            ev_ln3_mm = None
            ev_bc3 = None
            ev_s2_rd = None
            ev_s3_rd = None
            ev_a3 = None
            ev_nm3 = None
            for tc in range(4):
                tsl = slice(512 * tc, 512 * (tc + 1))
                for ot in range(16):
                    buf = ot % 2
                    wp_ev[buf] = K.dma(
                        "gpsimd",
                        wpb[buf].rearrange("p (t n) -> p t n", t=16)[:, :, :],
                        wq_p[:, :, 128 * ot:128 * (ot + 1)],
                        waits=(wp_free[buf],) + gate3,
                        sem="dW" if buf == 0 else "dWb")
                    wv = wpb[buf].rearrange("p (t n) -> p t n", t=16)
                    bank = ot % 4
                    ev_mm = None
                    for it in range(16):
                        ev_mm = K.t(lambda e, b=bank, it=it, wv=wv, tsl=tsl:
                                    e.matmul(
                            ps_p[b][:, :], wv[:, it, :], blkA[it][:, tsl],
                            start=(it == 0), stop=(it == 15)),
                            waits=((wp_ev[buf], pevac[bank], last_ctx_ev)
                                   if it == 0 else ()), sig=(it == 15))
                    wp_free[buf] = ev_mm
                    ev_e1 = K.s(lambda e, b=bank, ot=ot: e.add(
                        out=out1_v[:, ot, :], in_=ps_p[b][:, :],
                        add=bp_sb[:, ot:ot + 1]),
                        waits=(ev_mm, ev_bp, out1_free[ot]), sig=True)
                    ev_sq = K.s(lambda e, b=bank, ot=ot: e.activation(
                        out=sqb3[:, :], in_=ps_p[b][:, :], func=AF.Square,
                        bias=bp_sb[:, ot:ot + 1], scale=1.0),
                        waits=(ev_sqmm,), sig=True)
                    ev_o1b = K.v(lambda e, ot=ot: e.tensor_copy(
                        out=o1b[:, :], in_=out1_v[:, ot, :]),
                        waits=(ev_e1, ev_o1b_mm), sig=True)
                    ev_o1b_mm = K.t(lambda e, ot=ot: e.matmul(
                        ps_s2[0:1, :], ones128b[:, 0:1], o1b[:, :],
                        start=(ot == 0), stop=(ot == 15)),
                        waits=(ev_o1b, ev_s2_rd), sig=True)
                    ev_sqmm = K.t(lambda e, ot=ot: e.matmul(
                        ps_s2[32:33, :], ones128b[:, 0:1], sqb3[:, :],
                        start=(ot == 0), stop=(ot == 15)),
                        waits=(ev_sq,), sig=True)
                    pevac[bank] = ev_sq
                ev_r1 = K.v(lambda e: e.tensor_scalar(
                    out=r_p1[0:1, :], in0=ps_s2[0:1, :], scalar1=1.0 / H,
                    scalar2=None, op0=ALU.mult), waits=(ev_sqmm,), sig=True)
                ev_r1S = K.s(lambda e: e.nop(), waits=(ev_r1,), sig=True)
                ev_r2 = K.v(lambda e: e.scalar_tensor_tensor(
                    out=r_p2[0:1, :], in0=r_p1[0:1, :], scalar=-1.0,
                    in1=r_p1[0:1, :], op0=ALU.mult, op1=ALU.mult),
                    waits=(ev_r1S,), sig=True)
                ev_s2_rd = K.v(lambda e: e.scalar_tensor_tensor(
                    out=r_p2[0:1, :], in0=ps_s2[32:33, :], scalar=1.0 / H,
                    in1=r_p2[0:1, :], op0=ALU.mult, op1=ALU.add),
                    waits=(ev_r2,), sig=True)
                ev_r4 = K.s(lambda e: e.activation(
                    out=r_p2[0:1, :], in_=r_p2[0:1, :], func=AF.Ln,
                    bias=cst_eps[0:1, 0:1], scale=1.0), waits=(ev_s2_rd,),
                    sig=True)
                ev_r4V = K.v(lambda e: e.engine_nop(),
                             waits=(ev_r4,), sig=True)
                ev_r5 = K.s(lambda e: e.activation(
                    out=r_p2[0:1, :], in_=r_p2[0:1, :], func=AF.Exp,
                    scale=-0.5), waits=(ev_r4V,), sig=True)
                ev_r6 = K.v(lambda e: e.tensor_scalar(
                    out=r_p1[0:1, :], in0=r_p1[0:1, :], scalar1=-1.0,
                    scalar2=None, op0=ALU.mult), waits=(ev_r1S,), sig=True)
                ev_bm1 = K.t(lambda e: e.matmul(
                    ps_b3[:, 0:512], ones1f[0:1, :], r_p2[0:1, :],
                    start=True, stop=True), waits=(ev_r5, ev_bc3), sig=True)
                ev_bm2 = K.t(lambda e: e.matmul(
                    ps_b3[:, 512:1024], ones1f[0:1, :], r_p1[0:1, :],
                    start=True, stop=True), waits=(ev_r6,), sig=True)
                ev_c1 = K.s(lambda e: e.copy(
                    out=abc_sb[:, :], in_=ps_b3[:, 0:512]),
                    waits=(ev_bm1,), sig=True)
                ev_bc3 = K.s(lambda e: e.copy(
                    out=nbc_sb[:, :], in_=ps_b3[:, 512:1024]),
                    waits=(ev_bm2,), sig=True)
                ev_x2sq = None
                ev_x2b_prev = None
                for ot in range(16):
                    buf = ot % 2
                    x1t = [x1t0, x1t1][buf]
                    x2t = [x2t0, x2t1][buf]
                    ev_x1 = K.dma("sync", x1t[:, :], xT_v2[:, ot, tsl],
                                  waits=(x1_free[buf],) + gate3,
                                  sem="dX" if buf == 0 else "dXb")
                    ev_t1 = K.v(lambda e, ot=ot, x2t=x2t: e.tensor_tensor(
                        out=x2t[:, :], in0=out1_v[:, ot, :], in1=nbc_sb[:, :],
                        op=ALU.add),
                        waits=(ev_bc3, x2dma[buf], ev_x2sq), sig=True)
                    out1_free[ot] = ev_t1
                    ev_t2 = K.v(lambda e, x2t=x2t: e.tensor_tensor(
                        out=x2t[:, :], in0=x2t[:, :], in1=abc_sb[:, :],
                        op=ALU.mult), waits=(ev_c1,), sig=True)
                    ev_t3 = K.v(lambda e, x2t=x2t, x1t=x1t: e.tensor_tensor(
                        out=x2t[:, :], in0=x2t[:, :], in1=x1t[:, :],
                        op=ALU.add), waits=(ev_x1,), sig=True)
                    x1_free[buf] = ev_t3
                    x2dma[buf] = K.dma(
                        "gpsimd", x2scr_v[:, ot, tsl], x2t[:, :],
                        waits=(ev_t3,))
                    ev_x2b = K.v(lambda e, x2t=x2t: e.tensor_copy(
                        out=x2b[:, :], in_=x2t[:, :]),
                        waits=(ev_x2b_prev,), sig=True)
                    ev_x2sq = K.s(lambda e, x2t=x2t: e.activation(
                        out=sqb3[:, :], in_=x2t[:, :], func=AF.Square),
                        waits=(ev_ln3_mm, ev_t3), sig=True)
                    ev_x2b_prev = K.t(lambda e, ot=ot: e.matmul(
                        ps_s3[0:1, :], ones128b[:, 0:1], x2b[:, :],
                        start=(ot == 0), stop=(ot == 15)),
                        waits=(ev_x2b, ev_s3_rd), sig=True)
                    ev_ln3_mm = K.t(lambda e, ot=ot: e.matmul(
                        ps_s3[32:33, :], ones128b[:, 0:1], sqb3[:, :],
                        start=(ot == 0), stop=(ot == 15)),
                        waits=(ev_x2sq,), sig=True)
                ev_q1 = K.v(lambda e: e.tensor_scalar(
                    out=r_p1[0:1, :], in0=ps_s3[0:1, :], scalar1=1.0 / H,
                    scalar2=None, op0=ALU.mult), waits=(ev_ln3_mm,), sig=True)
                ev_q1S = K.s(lambda e: e.nop(), waits=(ev_q1,), sig=True)
                ev_q2 = K.v(lambda e: e.scalar_tensor_tensor(
                    out=r_p2[0:1, :], in0=r_p1[0:1, :], scalar=-1.0,
                    in1=r_p1[0:1, :], op0=ALU.mult, op1=ALU.mult),
                    waits=(ev_q1S,), sig=True)
                ev_s3_rd = K.v(lambda e: e.scalar_tensor_tensor(
                    out=r_p2[0:1, :], in0=ps_s3[32:33, :], scalar=1.0 / H,
                    in1=r_p2[0:1, :], op0=ALU.mult, op1=ALU.add),
                    waits=(ev_q2,), sig=True)
                ev_q4 = K.s(lambda e: e.activation(
                    out=r_p2[0:1, :], in_=r_p2[0:1, :], func=AF.Ln,
                    bias=cst_eps[0:1, 0:1], scale=1.0), waits=(ev_s3_rd,),
                    sig=True)
                ev_q4V = K.v(lambda e: e.engine_nop(),
                             waits=(ev_q4,), sig=True)
                ev_q5 = K.s(lambda e, tsl=tsl: e.activation(
                    out=r_a3f[0:1, tsl], in_=r_p2[0:1, :], func=AF.Exp,
                    scale=-0.5), waits=(ev_q4V,), sig=True)
                ev_a3 = ev_q5
                ev_q6 = K.v(lambda e: e.tensor_scalar(
                    out=r_p1[0:1, :], in0=r_p1[0:1, :], scalar1=-1.0,
                    scalar2=None, op0=ALU.mult), waits=(ev_q1S,), sig=True)
                ev_nm3 = K.v(lambda e, tsl=tsl: e.tensor_copy(
                    out=r_nm3b[0:1, tsl], in_=r_p1[0:1, :]),
                    waits=(ev_q6,), sig=True)
            gate4 = (ev_a3, ev_nm3, K.dma_ev("gpsimd"), K.dma_ev("dX"))

        blkA_scope.close()

        if "d_x2" in dbg:
            K.dma("gpsimd", dbg["d_x2"], x2scr, waits=gate4)

        K.enabled = KSTAGE >= 6
        _sc5 = ExitStack()
        with _sc5 as _s5:
            x2bf = _s5.enter_context(nc.sbuf_tensor("x2bf", [128, 16 * 1024], BF16))
            h_sb = _s5.enter_context(nc.sbuf_tensor("h_sb", [128, 44 * 1024], BF16))
            w_sb0 = _s5.enter_context(nc.sbuf_tensor("w_sb0", [128, 2048], BF16))
            w_sb1 = _s5.enter_context(nc.sbuf_tensor("w_sb1", [128, 2048], BF16))
            wsr0 = _s5.enter_context(nc.sbuf_tensor("wsr0", [1, 128], BF16))
            wsr1 = _s5.enter_context(nc.sbuf_tensor("wsr1", [1, 128], BF16))
            dn0 = _s5.enter_context(nc.sbuf_tensor("dn0", [128, 512], BF16))
            dn1 = _s5.enter_context(nc.sbuf_tensor("dn1", [128, 512], BF16))
            g1t = _s5.enter_context(nc.sbuf_tensor("g1", [128, 1024], F32))
            u1t = _s5.enter_context(nc.sbuf_tensor("u1", [128, 1024], F32))
            sgt = _s5.enter_context(nc.sbuf_tensor("sg", [128, 1024], F32))
            ybf = _s5.enter_context(nc.sbuf_tensor("ybf", [128, 1024], BF16))
            yf0 = _s5.enter_context(nc.sbuf_tensor("yf0", [128, 1024], F32))
            yf1 = _s5.enter_context(nc.sbuf_tensor("yf1", [128, 1024], F32))
            sqb4 = _s5.enter_context(nc.sbuf_tensor("sqb4", [128, 1024], BF16))
            r_f1 = _s5.enter_context(nc.sbuf_tensor("r_f1", [1, 1024], F32))
            r_f2 = _s5.enter_context(nc.sbuf_tensor("r_f2", [1, 1024], F32))
            r_f1b = _s5.enter_context(nc.sbuf_tensor("r_f1b", [1, 1024], BF16))
            r_f2b = _s5.enter_context(nc.sbuf_tensor("r_f2b", [1, 1024], BF16))
            a4bc = _s5.enter_context(nc.sbuf_tensor("a4bc", [128, 1024], F32))
            n4bc = _s5.enter_context(nc.sbuf_tensor("n4bc", [128, 1024], F32))
            ot0 = _s5.enter_context(nc.sbuf_tensor("ot0", [128, 1024], F32))
            ot1 = _s5.enter_context(nc.sbuf_tensor("ot1", [128, 1024], F32))
            ps_f0 = _s5.enter_context(nc.psum_tensor("ps_f0", [128, 1024], F32))
            ps_f1 = _s5.enter_context(nc.psum_tensor("ps_f1", [128, 1024], F32))
            ps_y0 = _s5.enter_context(nc.psum_tensor("ps_y0", [33, 512], F32))
            ps_y1 = _s5.enter_context(nc.psum_tensor("ps_y1", [33, 512], F32))
            ps_b4 = _s5.enter_context(nc.psum_tensor("ps_b4", [128, 1024], F32))
            ps_f = [ps_f0, ps_f1]
            wsb = [w_sb0, w_sb1]
            wsrb = [wsr0, wsr1]
            dnb = [dn0, dn1]
            x2bf_v = x2bf.rearrange("p (t n) -> p t n", t=16)
            h_v = h_sb.rearrange("p (t n) -> p t n", t=44)
            x2scr_v2 = x2scr.rearrange("(t p) n -> p t n", p=128)
            yscr_v = yscr.rearrange("(t p) n -> p t n", p=128)
            out_v = ap_out.rearrange("(t p) n -> p t n", p=128)

            prev_down_done = None
            prev_h_last = None
            prev_stat_rd = None
            ev_b4_free = None
            prev_final_rd = None
            for tc2 in range(2):
                t2sl = slice(1024 * tc2, 1024 * (tc2 + 1))
                ev_x2l = None
                for ht in range(16):
                    ev_x2l = K.dma(
                        "gpsimd", x2bf_v[:, ht, :], x2scr_v2[:, ht, t2sl],
                        waits=(gate4 + (prev_h_last,)) if ht == 0 else (),
                        sem="dW")
                ev_fb1 = K.t(lambda e, tc2=tc2: e.matmul(
                    ps_b4[:, 0:512], ones1f[0:1, :],
                    r_a3f[0:1, 1024 * tc2:1024 * tc2 + 512],
                    start=True, stop=True),
                    waits=(ev_a3, ev_b4_free), sig=True)
                ev_fb2 = K.t(lambda e, tc2=tc2: e.matmul(
                    ps_b4[:, 512:1024], ones1f[0:1, :],
                    r_a3f[0:1, 1024 * tc2 + 512:1024 * (tc2 + 1)],
                    start=True, stop=True), sig=True)
                ev_a3bc = K.s(lambda e: e.copy(
                    out=a4bc[:, :], in_=ps_b4[:, :]),
                    waits=(ev_fb1, ev_fb2, prev_final_rd), sig=True)

                w_free2 = [list(gate4), list(gate4)]
                fevac = [None, None]
                ev_h = None
                ev_sil = None
                for it in range(44):
                    evs_gu = []
                    for half, jo in ((0, it), (1, 44 + it)):
                        rk, oi = jo % 8, jo // 8
                        pbank = half
                        wbuf = half
                        base = rk * RT + WOFF + oi * 4 * 128
                        wf2 = w_free2[wbuf]
                        wf2t = tuple(wf2) if isinstance(wf2, list) else (wf2,)
                        ev_ws2 = K.dma(
                            "sync", wsrb[wbuf][0:1, :],
                            agout[rk * RT + WSOFF + oi:
                                  rk * RT + WSOFF + oi + 1, 0:128],
                            waits=(ev_cc,) + wf2t,
                            sem="dWS" if wbuf == 0 else "dWSb")
                        ev_wl = [None] * 4
                        for hg in range(4):
                            ev_wl[hg] = K.dma(
                                "sync",
                                wsb[wbuf][:, 512 * hg:512 * (hg + 1)],
                                agout[base + hg * 128:
                                      base + (hg + 1) * 128, :],
                                waits=wf2t if hg == 0 else (),
                                sem="dWB" if wbuf == 0 else "dWBb")
                        ev_mm = None
                        for hg in range(4):
                            for hs in range(4):
                                ht = hg * 4 + hs
                                for nn in range(2):
                                    ev_mm = K.t(
                                        lambda e, pb=pbank, ht=ht, hg=hg,
                                        hs=hs, nn=nn, wbuf=wbuf: e.matmul(
                                            ps_f[pb][:, 512 * nn:
                                                     512 * (nn + 1)],
                                            wsb[wbuf][:, 512 * hg + 128 * hs:
                                                      512 * hg +
                                                      128 * (hs + 1)],
                                            x2bf_v[:, ht, 512 * nn:
                                                   512 * (nn + 1)],
                                            start=(ht == 0), stop=False),
                                        waits=((ev_wl[hg], fevac[pbank],
                                                ev_x2l, prev_down_done)
                                               if hs == 0 and nn == 0
                                               else ()))
                        ev_f = None
                        for nn in range(2):
                            ev_f = K.t(lambda e, pb=pbank, nn=nn, wbuf=wbuf,
                                       tc2=tc2: e.matmul(
                                ps_f[pb][:, 512 * nn:512 * (nn + 1)],
                                wsrb[wbuf][0:1, :],
                                r_nm3b[0:1, 1024 * tc2 + 512 * nn:
                                       1024 * tc2 + 512 * (nn + 1)],
                                start=False, stop=True),
                                waits=(ev_nm3,) if nn == 0 else (),
                                sig=(nn == 1))
                        w_free2[wbuf] = ev_f
                        evs_gu.append(ev_f)
                    ev_g1 = K.v(lambda e: e.tensor_tensor(
                        out=g1t[:, :], in0=ps_f[0][:, :], in1=a4bc[:, :],
                        op=ALU.mult), waits=(evs_gu[0], ev_a3bc, ev_sil),
                        sig=True)
                    ev_u1 = K.v(lambda e: e.tensor_tensor(
                        out=u1t[:, :], in0=ps_f[1][:, :], in1=a4bc[:, :],
                        op=ALU.mult), waits=(evs_gu[1], ev_h), sig=True)
                    fevac[0] = ev_g1
                    fevac[1] = ev_u1
                    ev_sil = K.s(lambda e: e.activation(
                        out=sgt[:, :], in_=g1t[:, :], func=AF.Silu),
                        waits=(ev_g1, ev_h), sig=True)
                    ev_h = K.g(lambda e, it=it: e.tensor_tensor(
                        out=h_v[:, it, :], in0=sgt[:, :], in1=u1t[:, :],
                        op=ALU.mult),
                        waits=(ev_sil, ev_u1, prev_down_done), sig=True)
                prev_h_last = ev_h

                if KSTAGE < 7:
                    continue
                ev_ystat1 = None
                ev_ystat2 = None
                ev_ybf = None
                ev_ysq = None
                dn_ev = [None, None]
                dn_free = [None, None]
                ev_yd = [None, None]
                ev_down_last = None
                for hot in range(16):
                    rk, hc = hot // 2, hot % 2
                    pbank = hot % 2
                    base = rk * RT + DOFF + hc * 11 * 128
                    ev_mm = None
                    for ig in range(11):
                        db = ig % 2
                        dn_ev[db] = K.dma(
                            "sync", dnb[db][:, :],
                            agout[base + ig * 128:base + (ig + 1) * 128, :],
                            waits=(dn_free[db],) + (
                                (prev_h_last,) if hot == 0 and ig == 0
                                else ()), sem="dWB" if db == 0 else "dWBb")
                        for isub in range(4):
                            it = ig * 4 + isub
                            for nn in range(2):
                                ev_mm = K.t(
                                    lambda e, pb=pbank, it=it, isub=isub,
                                    nn=nn, db=db: e.matmul(
                                        ps_f[pb][:, 512 * nn:512 * (nn + 1)],
                                        dnb[db][:, 128 * isub:
                                                128 * (isub + 1)],
                                        h_v[:, it, 512 * nn:512 * (nn + 1)],
                                        start=(it == 0),
                                        stop=(it == 43)),
                                    waits=((dn_ev[db], fevac[pbank],
                                            prev_h_last)
                                           if isub == 0 and nn == 0 else ()),
                                    sig=(isub == 3 and nn == 1))
                        dn_free[db] = ev_mm
                    ev_down_last = ev_mm
                    yt = [yf0, yf1][hot % 2]
                    ev_yf = K.s(lambda e, pb=pbank, yt=yt: e.copy(
                        out=yt[:, :], in_=ps_f[pb][:, :]),
                        waits=(ev_mm, ev_yd[hot % 2], prev_final_rd),
                        sig=True)
                    ev_yd[hot % 2] = K.dma(
                        "gpsimd", yscr_v[:, hot, t2sl], yt[:, :],
                        waits=(ev_yf,), sem="dY" if tc2 == 0 else "dYb")
                    if KSUB < 2:
                        fevac[pbank] = ev_yf
                        continue
                    ev_ybf = K.v(lambda e, yt=yt: e.tensor_copy(
                        out=ybf[:, :], in_=yt[:, :]),
                        waits=(ev_yf, ev_ystat1), sig=True)
                    if KSUB < 3:
                        fevac[pbank] = ev_ybf
                        continue
                    ev_ysq = K.s(lambda e, yt=yt: e.activation(
                        out=sqb4[:, :], in_=yt[:, :], func=AF.Square),
                        waits=(ev_ystat2, ev_yf), sig=True)
                    for nn in range(2):
                        ev_ystat1 = K.t(lambda e, hot=hot, nn=nn: e.matmul(
                            [ps_y0, ps_y1][nn][0:1, :], ones128b[:, 0:1],
                            ybf[:, 512 * nn:512 * (nn + 1)],
                            start=(hot == 0), stop=(hot == 15)),
                            waits=(ev_ybf, prev_stat_rd), sig=True)
                        ev_ystat2 = K.t(lambda e, hot=hot, nn=nn: e.matmul(
                            [ps_y0, ps_y1][nn][32:33, :], ones128b[:, 0:1],
                            sqb4[:, 512 * nn:512 * (nn + 1)],
                            start=(hot == 0), stop=(hot == 15)),
                            waits=(ev_ysq,), sig=True)
                    fevac[pbank] = ev_yf
                prev_down_done = ev_down_last

                if KSTAGE < 8:
                    continue
                ev_e2 = None
                for nn in range(2):
                    nsl = slice(512 * nn, 512 * (nn + 1))
                    pz = [ps_y0, ps_y1][nn]
                    ev_c1m = K.v(lambda e, pz=pz, nsl=nsl: e.tensor_scalar(
                        out=r_f1[0:1, nsl], in0=pz[0:1, :], scalar1=1.0 / H,
                        scalar2=None, op0=ALU.mult),
                        waits=(ev_ystat1, ev_ystat2), sig=True)
                    ev_c1S = K.s(lambda e: e.nop(), waits=(ev_c1m,),
                                 sig=True)
                    ev_c2m = K.v(lambda e, nsl=nsl: e.scalar_tensor_tensor(
                        out=r_f2[0:1, nsl], in0=r_f1[0:1, nsl], scalar=-1.0,
                        in1=r_f1[0:1, nsl], op0=ALU.mult, op1=ALU.mult),
                        waits=(ev_c1S,), sig=True)
                    ev_e2 = K.v(lambda e, pz=pz, nsl=nsl:
                                e.scalar_tensor_tensor(
                        out=r_f2[0:1, nsl], in0=pz[32:33, :],
                        scalar=(1.0 + EPS_LN) / H, in1=r_f2[0:1, nsl],
                        op0=ALU.mult, op1=ALU.add), waits=(ev_c2m,),
                        sig=True)
                prev_stat_rd = ev_e2
                ev_e3 = K.s(lambda e: e.activation(
                    out=r_f2[0:1, :], in_=r_f2[0:1, :], func=AF.Ln,
                    bias=cst_eps2[0:1, 0:1], scale=1.0), waits=(ev_e2,),
                    sig=True)
                ev_e3V = K.v(lambda e: e.engine_nop(),
                             waits=(ev_e3,), sig=True)
                ev_e5 = K.s(lambda e: e.activation(
                    out=r_f2[0:1, :], in_=r_f2[0:1, :], func=AF.Exp,
                    scale=-0.5), waits=(ev_e3V,), sig=True)
                ev_e2S = K.s(lambda e: e.nop(), waits=(ev_e2,), sig=True)
                ev_e7 = K.v(lambda e: e.tensor_scalar(
                    out=r_f1[0:1, :], in0=r_f1[0:1, :], scalar1=-1.0,
                    scalar2=None, op0=ALU.mult), waits=(ev_e2S,), sig=True)
                ev_g1m = K.t(lambda e: e.matmul(
                    ps_b4[:, 0:512], ones1f[0:1, :], r_f2[0:1, 0:512],
                    start=True, stop=True), waits=(ev_e5, ev_a3bc), sig=True)
                ev_g2m = K.t(lambda e: e.matmul(
                    ps_b4[:, 512:1024], ones1f[0:1, :], r_f2[0:1, 512:1024],
                    start=True, stop=True), sig=True)
                ev_a4 = K.s(lambda e: e.copy(
                    out=a4bc[:, :], in_=ps_b4[:, :]),
                    waits=(ev_g1m, ev_g2m), sig=True)
                ev_g3m = K.t(lambda e: e.matmul(
                    ps_b4[:, 0:512], ones1f[0:1, :], r_f1[0:1, 0:512],
                    start=True, stop=True), waits=(ev_e7, ev_a4), sig=True)
                ev_g4m = K.t(lambda e: e.matmul(
                    ps_b4[:, 512:1024], ones1f[0:1, :], r_f1[0:1, 512:1024],
                    start=True, stop=True), sig=True)
                ev_n4 = K.s(lambda e: e.copy(
                    out=n4bc[:, :], in_=ps_b4[:, :]),
                    waits=(ev_g3m, ev_g4m), sig=True)
                ev_b4_free = ev_n4

                dg_after_down = K.dma_ev("dY" if tc2 == 0 else "dYb")
                ev_outd = [None, None]
                ev_f3p = [None, None]
                for ht in range(16):
                    buf = ht % 2
                    yt = [yf0, yf1][buf]
                    x2t = [ot0, ot1][buf]
                    ev_yl = K.dma(
                        "gpsimd", yt[:, :], yscr_v[:, ht, t2sl],
                        waits=(dg_after_down, ev_yf, ev_f3p[buf]),
                        sem="dY2" if buf == 0 else "dY2b")
                    ev_xl = K.dma("sync", x2t[:, :], x2scr_v2[:, ht, t2sl],
                                  waits=(ev_outd[buf], gate4[2]),
                                  sem="dX" if buf == 0 else "dXb")
                    ev_f1 = K.v(lambda e, yt=yt: e.tensor_tensor(
                        out=yt[:, :], in0=yt[:, :], in1=n4bc[:, :],
                        op=ALU.add), waits=(ev_yl, ev_n4), sig=True)
                    ev_f2 = K.v(lambda e, yt=yt: e.tensor_tensor(
                        out=yt[:, :], in0=yt[:, :], in1=a4bc[:, :],
                        op=ALU.mult), waits=(ev_f1,), sig=True)
                    ev_f3 = K.v(lambda e, yt=yt, x2t=x2t: e.tensor_tensor(
                        out=x2t[:, :], in0=x2t[:, :], in1=yt[:, :],
                        op=ALU.add), waits=(ev_f2, ev_xl), sig=True)
                    ev_f3p[buf] = ev_f3
                    prev_final_rd = ev_f3
                    ev_outd[buf] = K.dma(
                        "scalar", out_v[:, ht, t2sl], x2t[:, :],
                        waits=(ev_f3,))

        if "d_y" in dbg:
            K.dma("gpsimd", dbg["d_y"], yscr,
                  waits=(K.dma_ev("dY"), K.dma_ev("dYb")))
        K.enabled = True
        K.wait_only("sync", K.all_dma_evs())

        with nc.Block() as block:
            K.build(block, sems)

    return nc


_NC_CACHE = {}


def kernel(**inputs):
    in_maps = stage_inputs(inputs)
    key = tuple(sorted(n for n, _, _ in DEBUG_TAPS))
    if key not in _NC_CACHE:
        _NC_CACHE[key] = build_program()
    nc = _NC_CACHE[key]
    res = run_bass_kernel_spmd(nc, in_maps, core_ids=list(range(NCORES)))
    kernel.last_results = res.results
    return assemble_output(res.results)


# version 05
# speedup vs baseline: 1.0635x; 1.0635x over previous
import math
import numpy as np
from contextlib import ExitStack

import concourse.bass as bass
import concourse.mybir as mybir
from concourse.bass_utils import run_bass_kernel_spmd

F32 = mybir.dt.float32
BF16 = mybir.dt.bfloat16
I32 = mybir.dt.int32
F16 = mybir.dt.float16
AF = mybir.ActivationFunctionType
ALU = mybir.AluOpType
AX = mybir.AxisListType

NCORES = 8
B, S, H = 4, 4096, 2048
NH, HN = 16, 128
NT = 2048
SL = S // NCORES
NG = 64
FI = 5632
EPS_LN = 1e-5
EPS_RMS = 1e-6
RSQ = 1.0 / math.sqrt(128.0)

KOFF = 0
VOFF = KOFF + NG * 128
NOFF = VOFF + NG * 128
RT1 = NOFF + 1
WOFF = 0
DOFF = WOFF + 11 * 4 * 128
WSOFF = DOFF + 2 * 11 * 128
RT2 = WSOFF + 13
RT = RT1 + RT2

DEBUG_TAPS = []
import os as _os
KSTAGE = int(_os.environ.get("KSTAGE", "9"))
KSUB = int(_os.environ.get("KSUB", "9"))
KOWNWAITS = int(_os.environ.get("KOWNWAITS", "0"))

ENG_LIST = ["sync", "gpsimd", "vector", "scalar", "tensor"]
DMA_SEM = {"sync": "dS", "gpsimd": "dG", "scalar": "dA"}
CMP_SEM = {"gpsimd": "G", "vector": "V", "scalar": "S", "tensor": "T"}
DMA_STREAM_SEMS = (["dX", "dXb", "dPK", "dPKb", "dKV", "dKVb",
                     "dWB", "dWBb", "dWS", "dWSb", "dW", "dWb",
                     "dS2", "dMK", "dY", "dYb", "dY2", "dY2b"])
ALL_SEMS = ["T", "V", "S", "G", "dS", "dG", "dA", "CC"] + DMA_STREAM_SEMS


class KCtx:

    def __init__(self, nc):
        self.nc = nc
        self.items = []
        self.cnt = {s: 0 for s in ALL_SEMS}

    enabled = True

    def _emit(self, eng, builder, waits, sig_sem, amt):
        if not self.enabled:
            return (sig_sem, 0) if sig_sem is not None else None
        flat = []
        for w in waits:
            if w is None:
                continue
            flat.append(w)
        ev = None
        if sig_sem is not None:
            self.cnt[sig_sem] += amt
            ev = (sig_sem, self.cnt[sig_sem])
        self.items.append((eng, builder, tuple(flat), sig_sem, amt))
        return ev

    def t(self, builder, waits=(), sig=False):
        return self._emit("tensor", builder, waits, "T" if sig else None, 1)

    def v(self, builder, waits=(), sig=False):
        return self._emit("vector", builder, waits, "V" if sig else None, 1)

    def s(self, builder, waits=(), sig=False):
        return self._emit("scalar", builder, waits, "S" if sig else None, 1)

    def g(self, builder, waits=(), sig=False):
        return self._emit("gpsimd", builder, waits, "G" if sig else None, 1)

    def dma(self, q, out, in_, waits=(), sem=None):
        return self._emit(
            q, (lambda e, o=out, i=in_: e.dma_start(out=o, in_=i)),
            waits, sem if sem is not None else DMA_SEM[q], 16)

    def all_dma_evs(self):
        return tuple((s, self.cnt[s]) for s in
                     ["dS", "dG", "dA"] + DMA_STREAM_SEMS if self.cnt[s])

    def coll(self, builder, waits=()):
        return self._emit("gpsimd", builder, waits, "CC", 1)

    def wait_only(self, eng, waits):
        self._emit(eng, lambda e: e.engine_nop() if eng != "sync"
                   else e.nop(), waits, None, 0)

    def dma_ev(self, q):
        sem = DMA_SEM.get(q, q)
        return (sem, self.cnt[sem])

    def build(self, block, sems):
        def section(eng_name):
            own_sem = CMP_SEM.get(eng_name)

            def body(eng):
                hw = {s: 0 for s in ALL_SEMS}
                own_count = 0
                for (en, builder, waits, sig_sem, amt) in self.items:
                    if en != eng_name:
                        continue
                    for (ws, wv) in waits:
                        if ws == own_sem:
                            if wv <= own_count:
                                if KOWNWAITS:
                                    eng.wait_ge(sems[ws], wv)
                                continue
                            raise RuntimeError(
                                f"{eng_name} waits own future {ws}>={wv}")
                        if hw[ws] >= wv:
                            continue
                        eng.wait_ge(sems[ws], wv)
                        hw[ws] = wv
                    inst = builder(eng)
                    if sig_sem is not None:
                        inst.then_inc(sems[sig_sem], amt)
                        if sig_sem == own_sem:
                            own_count += amt
            return body

        for name in ENG_LIST:
            getattr(block, name)(section(name))


def stage_inputs(inp):
    x = np.ascontiguousarray(np.asarray(inp["L0_attn_query"]))
    Wqkv = np.asarray(inp["L0_attn_Wqkv"])
    Wproj = np.asarray(inp["L0_attn_Wproj"])
    bproj = np.asarray(inp["L0_attn_bproj"])
    gup = np.asarray(inp["L0_ffn_gate_up_packed"])
    gus = np.asarray(inp["L0_ffn_gate_up_scales"])
    dwp = np.asarray(inp["L0_ffn_down_packed"])
    dws = np.asarray(inp["L0_ffn_down_scales"])

    WqkvT = np.ascontiguousarray(Wqkv.T)
    WprojT = np.ascontiguousarray(Wproj.T)
    bproj2 = np.ascontiguousarray(bproj.reshape(16, 128).T)
    gusT = np.ascontiguousarray(gus.T)
    dwsT = np.ascontiguousarray(dws.T)

    in_maps = []
    for c in range(NCORES):
        sl = slice(SL * c, SL * (c + 1))
        xT = np.ascontiguousarray(
            x[:, sl, :].transpose(2, 0, 1).reshape(H, NT))
        rows = np.concatenate(
            [np.arange(128 * j, 128 * (j + 1)) for j in range(c, 88, 8)])
        in_maps.append({
            "xT": xT,
            "WqkvT": WqkvT,
            "WprojT": WprojT,
            "bproj": bproj2,
            "gup": np.ascontiguousarray(gup[rows, :]),
            "gus": np.ascontiguousarray(gusT[rows, :]),
            "dwp": np.ascontiguousarray(dwp[256 * c:256 * (c + 1), :]),
            "dws": np.ascontiguousarray(dwsT[256 * c:256 * (c + 1), :]),
        })
    return in_maps


def assemble_output(results):
    out = np.empty((B, S, H), dtype=np.float32)
    for c in range(NCORES):
        yT = np.asarray(results[c]["OUT"])
        out[:, SL * c:SL * (c + 1), :] = \
            yT.reshape(H, B, SL).transpose(1, 2, 0)
    return out


def build_program():
    nc = bass.Bass("TRN2", target_bir_lowering=False, debug=False)
    K = KCtx(nc)

    ap_xT = nc.dram_tensor("xT", [H, NT], F32, kind="ExternalInput").ap()
    ap_Wq = nc.dram_tensor("WqkvT", [H, 3 * H], F32, kind="ExternalInput").ap()
    ap_Wp = nc.dram_tensor("WprojT", [H, H], F32, kind="ExternalInput").ap()
    ap_bp = nc.dram_tensor("bproj", [128, 16], F32, kind="ExternalInput").ap()
    ap_gup = nc.dram_tensor("gup", [1408, 256], I32, kind="ExternalInput").ap()
    ap_gus = nc.dram_tensor("gus", [1408, 16], F16, kind="ExternalInput").ap()
    ap_dwp = nc.dram_tensor("dwp", [256, 704], I32, kind="ExternalInput").ap()
    ap_dws = nc.dram_tensor("dws", [256, 44], F16, kind="ExternalInput").ap()
    ap_out = nc.dram_tensor("OUT", [H, NT], F32, kind="ExternalOutput").ap()
    dbg = {}
    for name, shape, dt in DEBUG_TAPS:
        dbg[name] = nc.dram_tensor(name, shape, dt, kind="ExternalOutput").ap()

    agin = nc.dram_tensor("agin", [RT1, 512], BF16).ap()
    agout = nc.dram_tensor("agout", [NCORES * RT1, 512], BF16).ap()
    agin2 = nc.dram_tensor("agin2", [RT2, 512], BF16).ap()
    agout2 = nc.dram_tensor("agout2", [NCORES * RT2, 512], BF16).ap()
    qscr = nc.dram_tensor("qscr", [H, NT], BF16).ap()
    x2scr = nc.dram_tensor("x2scr", [H, NT], F32).ap()
    yscr = nc.dram_tensor("yscr", [H, NT], F32).ap()

    st = ExitStack()
    with st:
        sems = {s: st.enter_context(nc.semaphore("sem_" + s))
                for s in ALL_SEMS}

        blkA_scope = ExitStack()
        blkA = [blkA_scope.enter_context(
            nc.sbuf_tensor(f"A{i}", [128, NT], BF16, side="right"))
            for i in range(16)]
        ones128b = st.enter_context(nc.sbuf_tensor("ones128b", [128, 1], BF16))
        ones1b = st.enter_context(nc.sbuf_tensor("ones1b", [1, 128], BF16))
        ones1f = st.enter_context(nc.sbuf_tensor("ones1f", [1, 128], F32))
        ident = st.enter_context(nc.sbuf_tensor("ident", [128, 128], BF16))
        r_mq = st.enter_context(nc.sbuf_tensor("r_mq", [1, 64], F32))
        r_mkl = st.enter_context(nc.sbuf_tensor("r_mkl", [1, 64], F32))
        r_mkb = st.enter_context(nc.sbuf_tensor("r_mkb", [1, 64], BF16))
        r_a3f = st.enter_context(nc.sbuf_tensor("r_a3f", [1, NT], F32))
        r_nm3b = st.enter_context(nc.sbuf_tensor("r_nm3b", [1, NT], BF16))
        cst_eps = st.enter_context(nc.sbuf_tensor("cst_eps", [1, 1], F32))
        cst_eps2 = st.enter_context(nc.sbuf_tensor("cst_eps2", [1, 1], F32))

        K.g(lambda e: e.memset(ones128b[:, :], 1.0))
        K.g(lambda e: e.memset(cst_eps[:, :], EPS_LN))
        K.g(lambda e: e.memset(cst_eps2[:, :], EPS_RMS * EPS_LN))
        K.g(lambda e: e.memset(ones1b[:, :], 1.0))
        K.g(lambda e: e.memset(ones1f[:, :], 1.0))
        ev_memset = K.g(lambda e: e.memset(ident[:, :], 0.0), sig=True)
        ev_ident = K.g(lambda e: e.affine_select(
            out=ident[:, :], in_=ident[:, :], compare_op=ALU.not_equal,
            fill=1.0, base=0, pattern=[[-1, 128]], channel_multiplier=1),
            waits=(ev_memset,), sig=True)

        p1_scope = ExitStack()
        with p1_scope as p1:
            r_a1 = p1.enter_context(nc.sbuf_tensor("r_a1", [1, NT], F32))
            r_nm1 = p1.enter_context(nc.sbuf_tensor("r_nm1", [1, NT], F32))
            r_nm1b = p1.enter_context(nc.sbuf_tensor("r_nm1b", [1, NT], BF16))
            r_t1 = p1.enter_context(nc.sbuf_tensor("r_t1", [1, 512], F32))
            r_t2 = p1.enter_context(nc.sbuf_tensor("r_t2", [1, 512], F32))
            a1bc = [p1.enter_context(
                nc.sbuf_tensor(f"a1bc{i}", [128, 512], F32))
                for i in range(4)]

            _sc1 = ExitStack()
            with _sc1 as _s1:
                ps_st = _s1.enter_context(nc.psum_tensor("ps_st", [33, 512], F32))
                ps_bc = _s1.enter_context(nc.psum_tensor("ps_bc", [128, 512], F32))
                xf = _s1.enter_context(nc.sbuf_tensor("xf", [128, 16 * 512], F32))
                sqb = _s1.enter_context(nc.sbuf_tensor("sqb", [128, 512], BF16))
                r_a1b = _s1.enter_context(nc.sbuf_tensor("r_a1b", [1, NT], BF16))
                xf_v = xf.rearrange("p (t n) -> p t n", t=16)
                xT_v = ap_xT.rearrange("(t p) n -> p t n", p=128)
                ev_bc_prev = None
                ev_xbf_last = None
                ev_sq = None
                ev_st_rd = None
                for tc in range(4):
                    tsl = slice(512 * tc, 512 * (tc + 1))
                    ev_ld = K.dma("sync", xf_v[:, :, :], xT_v[:, :, tsl],
                                  waits=(ev_xbf_last, ev_sq), sem="dX")
                    ev_sq_mm = None
                    for ht in range(16):
                        ev_xbf_last = K.v(
                            lambda e, ht=ht, tsl=tsl: e.tensor_copy(
                                out=blkA[ht][:, tsl], in_=xf_v[:, ht, :]),
                            waits=(ev_ld,), sig=True)
                        ev_sq = K.s(lambda e, ht=ht: e.activation(
                            out=sqb[:, :], in_=xf_v[:, ht, :], func=AF.Square),
                            waits=(ev_ld, ev_sq_mm), sig=True)
                        K.t(lambda e, ht=ht, tsl=tsl: e.matmul(
                            ps_st[0:1, :], ones128b[:, 0:1], blkA[ht][:, tsl],
                            start=(ht == 0), stop=(ht == 15)),
                            waits=(ev_xbf_last, ev_st_rd, ev_ident))
                        ev_sq_mm = K.t(lambda e, ht=ht: e.matmul(
                            ps_st[32:33, :], ones128b[:, 0:1], sqb[:, :],
                            start=(ht == 0), stop=(ht == 15)),
                            waits=(ev_sq,), sig=True)
                        if ht == 15:
                            ev_sq_mm = K.t(lambda e: e.drain(), sig=True)
                    ev_mu = K.v(lambda e: e.tensor_scalar(
                        out=r_t1[0:1, :], in0=ps_st[0:1, :],
                        scalar1=1.0 / H, scalar2=None, op0=ALU.mult),
                        waits=(ev_sq_mm,), sig=True)
                    ev_nm = K.v(lambda e, tsl=tsl: e.tensor_scalar(
                        out=r_nm1[0:1, tsl], in0=ps_st[0:1, :],
                        scalar1=-1.0 / H, scalar2=None, op0=ALU.mult),
                        waits=(ev_sq_mm,), sig=True)
                    ev_muS = K.s(lambda e: e.nop(), waits=(ev_mu,),
                                 sig=True)
                    ev_m2 = K.v(lambda e: e.scalar_tensor_tensor(
                        out=r_t2[0:1, :], in0=r_t1[0:1, :], scalar=-1.0,
                        in1=r_t1[0:1, :], op0=ALU.mult, op1=ALU.mult),
                        waits=(ev_muS,), sig=True)
                    ev_st_rd = K.v(lambda e: e.scalar_tensor_tensor(
                        out=r_t1[0:1, :], in0=ps_st[32:33, :], scalar=1.0 / H,
                        in1=r_t2[0:1, :], op0=ALU.mult, op1=ALU.add),
                        waits=(ev_m2,), sig=True)
                    ev_sd = K.s(lambda e: e.activation(
                        out=r_t2[0:1, :], in_=r_t1[0:1, :], func=AF.Ln,
                        bias=cst_eps[0:1, 0:1], scale=1.0),
                        waits=(ev_st_rd, ev_ident), sig=True)
                    ev_sdV = K.v(lambda e: e.engine_nop(),
                                 waits=(ev_sd,), sig=True)
                    ev_a1 = K.s(lambda e, tsl=tsl: e.activation(
                        out=r_a1[0:1, tsl], in_=r_t2[0:1, :], func=AF.Exp,
                        scale=-0.5), waits=(ev_sdV,), sig=True)
                    ev_nmS = K.s(lambda e: e.nop(), waits=(ev_nm,),
                                 sig=True)
                    ev_nm1b = K.v(lambda e, tsl=tsl: e.tensor_copy(
                        out=r_nm1b[0:1, tsl], in_=r_nm1[0:1, tsl]),
                        waits=(ev_nmS,), sig=True)
                    K.t(lambda e, tsl=tsl: e.matmul(
                        ps_bc[:, :], ones1f[0:1, :], r_a1[0:1, tsl],
                        start=True, stop=True),
                        waits=(ev_a1, ev_bc_prev))
                    ev_bcm = K.t(lambda e: e.drain(), sig=True)
                    ev_bc_prev = K.s(lambda e, tc=tc: e.copy(
                        out=a1bc[tc][:, :], in_=ps_bc[:, :]),
                        waits=(ev_bcm,), sig=True)
                ev_rows1 = ev_bc_prev
                ev_nm1b_done = ev_nm1b

            K.enabled = KSTAGE >= 2
            if "d_xbf" in dbg:
                for htd in range(16):
                    K.dma("gpsimd", dbg["d_xbf"][128 * htd:128 * (htd + 1), :],
                          blkA[htd][:, :], waits=(ev_xbf_last, ev_rows1))
            if "d_rows" in dbg:
                K.dma("gpsimd", dbg["d_rows"][0:1, :], r_a1[0:1, :],
                      waits=(ev_rows1,))
                K.dma("gpsimd", dbg["d_rows"][1:2, :], r_nm1[0:1, :],
                      waits=(ev_rows1,))
            if "d_a1bc" in dbg:
                for tcd in range(4):
                    K.dma("gpsimd",
                          dbg["d_a1bc"][:, 512 * tcd:512 * (tcd + 1)],
                          a1bc[tcd][:, :], waits=(ev_rows1,))
            ev_mq_done = None
            _sc2 = ExitStack()
            with _sc2 as _s2:
                wst0 = _s2.enter_context(nc.sbuf_tensor("wst0", [128, 16 * 128], BF16))
                wst1 = _s2.enter_context(nc.sbuf_tensor("wst1", [128, 16 * 128], BF16))
                ods0 = _s2.enter_context(nc.sbuf_tensor("ods0", [128, NT], BF16))
                ods1 = _s2.enter_context(nc.sbuf_tensor("ods1", [128, NT], BF16))
                vts0 = _s2.enter_context(nc.sbuf_tensor("vts0", [128, NT], BF16))
                vts1 = _s2.enter_context(nc.sbuf_tensor("vts1", [128, NT], BF16))
                sqb2 = _s2.enter_context(nc.sbuf_tensor("sqb2", [128, NT], BF16))
                r_ws = _s2.enter_context(nc.sbuf_tensor("r_ws", [1, 128], BF16))
                r_nrm = _s2.enter_context(nc.sbuf_tensor("r_nrm", [1, NT], F32))
                pk0 = _s2.enter_context(nc.sbuf_tensor("pk0", [128, 704], I32))
                pk1 = _s2.enter_context(nc.sbuf_tensor("pk1", [128, 704], I32))
                dq_ni = _s2.enter_context(nc.sbuf_tensor("dq_ni", [128, 704], I32))
                dq_nf = _s2.enter_context(nc.sbuf_tensor("dq_nf", [128, 704], F32))
                dq_ind = _s2.enter_context(nc.sbuf_tensor("dq_ind", [128, 704], F32))
                dq_m = _s2.enter_context(nc.sbuf_tensor("dq_m", [128, 704], F32))
                dq_ta = _s2.enter_context(nc.sbuf_tensor("dq_ta", [128, 704], F32))
                dq_tb = _s2.enter_context(nc.sbuf_tensor("dq_tb", [128, 704], F32))
                wdeq = _s2.enter_context(nc.sbuf_tensor("wdeq", [128, FI], BF16))
                scl = _s2.enter_context(nc.sbuf_tensor("scl", [128, 44], F32))
                wTst = _s2.enter_context(nc.sbuf_tensor("wTst", [128, 512], BF16))
                r_wsum = _s2.enter_context(nc.sbuf_tensor("r_wsum", [1, 128], BF16))
                wsc = _s2.enter_context(nc.sbuf_tensor("wsc", [128, 1], F32))
                wscb = _s2.enter_context(nc.sbuf_tensor("wscb", [128, 1], BF16))
                ps_q0 = _s2.enter_context(nc.psum_tensor("ps_q0", [128, 512], F32))
                ps_q1 = _s2.enter_context(nc.psum_tensor("ps_q1", [128, 512], F32))
                ps_q2 = _s2.enter_context(nc.psum_tensor("ps_q2", [128, 512], F32))
                ps_q3 = _s2.enter_context(nc.psum_tensor("ps_q3", [128, 512], F32))
                ps_x1 = _s2.enter_context(nc.psum_tensor("ps_x1", [1, 512], F32))
                ps_x2 = _s2.enter_context(nc.psum_tensor("ps_x2", [1, 512], F32))
                ps_t0 = _s2.enter_context(nc.psum_tensor("ps_t0", [128, 128], BF16))
                ps_t1 = _s2.enter_context(nc.psum_tensor("ps_t1", [128, 128], BF16))
                p1a_gate = [ev_rows1, ev_nm1b_done, ev_sq_mm]
                ps_q = [ps_q0, ps_q1, ps_q2, ps_q3]
                wbufs = [wst0, wst1]
                odss = [ods0, ods1]
                vtss = [vts0, vts1]
                pks = [pk0, pk1]
                tr_ps = [ps_t0, ps_t1]
                tr_free = [ev_ident, ev_ident]
                w_ev = [None, None]
                w_free = [p1a_gate, p1a_gate]
                pk_ev = [None, None]
                pk_free = [p1a_gate, p1a_gate]
                evac_ev = [None, None, None, None]
                aux_ev = [None, None]
                ods_free = [[], []]
                vts_free = [[], []]
                ev_nrm_mm_last = None
                ev_dq_tr_last = None
                ev_wT_dma = None
                ev_wsum_dma = None
                ev_scl_free = None
                ev_mk_done = None
                wq_v = ap_Wq.rearrange("(t p) n -> p t n", p=128)

                dq_units = [("gu", i) for i in range(11)] + \
                           [("dn", i) for i in range(2)]
                dq_at = {}
                for i, u in enumerate(dq_units):
                    dq_at[2 + 3 * i] = (u, i)
                pk_ev[0] = K.dma("sync", pk0[:, 0:256], ap_gup[0:128, :],
                                 waits=tuple(p1a_gate), sem="dPK")

                def emit_dequant(unit, uidx):
                    nonlocal ev_scl_free, ev_dq_tr_last, ev_wT_dma, \
                        ev_wsum_dma
                    kind, idx = unit
                    C = 256 if kind == "gu" else 704
                    W = C * 8
                    ngrp = W // 128
                    buf = uidx % 2
                    pkb = pks[buf]
                    scl_w = ((ev_scl_free,) if ev_scl_free is not None
                             else tuple(p1a_gate))
                    if kind == "gu":
                        ev_s = K.dma("gpsimd", scl[:, 0:16],
                                     ap_gus[128 * idx:128 * (idx + 1), :],
                                     waits=scl_w, sem="dS2")
                    else:
                        ev_s = K.dma("gpsimd", scl[:, 0:44],
                                     ap_dws[128 * idx:128 * (idx + 1), :],
                                     waits=scl_w, sem="dS2")
                    wd_v = wdeq.rearrange("p (c k) -> p c k", k=8)
                    last_v = None
                    for kk in range(8):
                        K.v(lambda e, kk=kk, C=C: e.tensor_scalar(
                            out=dq_ni[:, 0:C], in0=pkb[:, 0:C],
                            scalar1=4 * kk, scalar2=15,
                            op0=ALU.logical_shift_right, op1=ALU.bitwise_and),
                            waits=((pk_ev[buf], ev_dq_tr_last)
                                   if kk == 0 else ()))
                        K.v(lambda e, C=C: e.tensor_copy(
                            out=dq_nf[:, 0:C], in_=dq_ni[:, 0:C]))
                        K.v(lambda e, C=C: e.tensor_scalar(
                            out=dq_ind[:, 0:C], in0=dq_nf[:, 0:C],
                            scalar1=7.5, scalar2=None, op0=ALU.is_ge))
                        K.v(lambda e, C=C: e.scalar_tensor_tensor(
                            out=dq_m[:, 0:C], in0=dq_ind[:, 0:C], scalar=-8.0,
                            in1=dq_nf[:, 0:C], op0=ALU.mult, op1=ALU.add))
                        K.v(lambda e, C=C: e.tensor_scalar(
                            out=dq_ta[:, 0:C], in0=dq_m[:, 0:C],
                            scalar1=0.5, scalar2=None, op0=ALU.mult))
                        K.v(lambda e, C=C: e.tensor_scalar(
                            out=dq_tb[:, 0:C], in0=dq_m[:, 0:C],
                            scalar1=2.0, scalar2=None, op0=ALU.subtract))
                        K.v(lambda e, C=C: e.tensor_tensor(
                            out=dq_ta[:, 0:C], in0=dq_ta[:, 0:C],
                            in1=dq_tb[:, 0:C], op=ALU.max))
                        K.v(lambda e, C=C: e.tensor_scalar(
                            out=dq_tb[:, 0:C], in0=dq_m[:, 0:C],
                            scalar1=2.0, scalar2=8.0,
                            op0=ALU.mult, op1=ALU.subtract))
                        K.v(lambda e, C=C: e.tensor_tensor(
                            out=dq_ta[:, 0:C], in0=dq_ta[:, 0:C],
                            in1=dq_tb[:, 0:C], op=ALU.max))
                        K.v(lambda e, C=C: e.tensor_scalar(
                            out=dq_ind[:, 0:C], in0=dq_ind[:, 0:C],
                            scalar1=-2.0, scalar2=1.0,
                            op0=ALU.mult, op1=ALU.add))
                        last_v = K.v(lambda e, kk=kk, C=C: e.tensor_tensor(
                            out=wd_v[:, 0:C, kk], in0=dq_ta[:, 0:C],
                            in1=dq_ind[:, 0:C], op=ALU.mult),
                            sig=(kk == 7))
                    pk_free[buf] = last_v
                    if uidx + 1 < len(dq_units):
                        nk, ni = dq_units[uidx + 1]
                        nbuf = (uidx + 1) % 2
                        src = (ap_gup[128 * ni:128 * (ni + 1), :]
                               if nk == "gu"
                               else ap_dwp[128 * ni:128 * (ni + 1), :])
                        ncols = 256 if nk == "gu" else 704
                        pfb = pk_free[nbuf]
                        pk_ev[nbuf] = K.dma(
                            "sync", pks[nbuf][:, 0:ncols], src,
                            waits=(tuple(pfb) if isinstance(pfb, list)
                                   else (pfb,)),
                            sem="dPK" if nbuf == 0 else "dPKb")
                    ev_scaled = None
                    for gi in range(ngrp):
                        ev_scaled = K.v(lambda e, gi=gi: e.tensor_scalar(
                            out=wdeq[:, 128 * gi:128 * (gi + 1)],
                            in0=wdeq[:, 128 * gi:128 * (gi + 1)],
                            scalar1=scl[:, gi:gi + 1], scalar2=None,
                            op0=ALU.mult),
                            waits=(last_v, ev_s) if gi == 0 else (),
                            sig=(gi == ngrp - 1))
                    ev_scl_free = ev_scaled
                    if uidx == 5 and "d_wdeq" in dbg:
                        ev_scl_free = K.dma("gpsimd", dbg["d_wdeq"],
                                            wdeq[:, :], waits=(ev_scaled,))
                        K.dma("gpsimd", dbg["d_scl"], scl[:, :],
                              waits=(ev_scaled,))
                    ev_red = K.v(lambda e, W=W: e.tensor_reduce(
                        out=wsc[:, 0:1], in_=wdeq[:, 0:W], axis=AX.X,
                        op=ALU.add), waits=(ev_scaled,), sig=True)
                    ev_redS = K.s(lambda e: e.nop(),
                                  waits=(ev_red,), sig=True)
                    ev_redb = K.v(lambda e: e.tensor_copy(
                        out=wscb[:, 0:1], in_=wsc[:, 0:1]),
                        waits=(ev_redS, ev_wsum_dma), sig=True)
                    row = (WSOFF + idx) if kind == "gu" else (WSOFF + 11 + idx)
                    ev_wsum_dma = K.dma("scalar", agin2[row:row + 1, 0:128],
                                        wscb[:, 0:1], waits=(ev_redb,))
                    ntile = W // 128
                    for it in range(ntile):
                        pp = tr_ps[it % 2]
                        K.t(lambda e, it=it, pp=pp: e.transpose(
                            pp[:, :], wdeq[:, 128 * it:128 * (it + 1)],
                            ident[:, :]),
                            waits=(ev_scaled, tr_free[it % 2]))
                        ev_tr = K.t(lambda e: e.drain(), sig=True)
                        ev_ev = K.v(lambda e, it=it, pp=pp: e.tensor_copy(
                            out=wTst[:, 128 * (it % 4):128 * (it % 4 + 1)],
                            in_=pp[:, :]),
                            waits=(ev_tr, ev_wT_dma), sig=True)
                        tr_free[it % 2] = ev_ev
                        ev_dq_tr_last = ev_tr
                        if it % 4 == 3:
                            if kind == "gu":
                                rowb = WOFF + (idx * 4 + it // 4) * 128
                            else:
                                rowb = DOFF + (idx * 11 + it // 4) * 128
                            ev_wT_dma = K.dma(
                                "scalar", agin2[rowb:rowb + 128, :],
                                wTst[:, :], waits=(ev_ev,))

                if "d_w0" in dbg:
                    pass
                od_order = ([3 * h + 1 for h in range(16)] +
                            [3 * h + 2 for h in range(16)] +
                            [3 * h for h in range(16)])
                ev_cc1 = None
                for oi_seq, ot in enumerate(od_order):
                    buf = oi_seq % 2
                    head, part = ot // 3, ot % 3
                    wv = wbufs[buf].rearrange("p (t n) -> p t n", t=16)
                    wfb = w_free[buf]
                    w_ev[buf] = K.dma(
                        "gpsimd", wv[:, :, :],
                        wq_v[:, :, 128 * ot:128 * (ot + 1)],
                        waits=(tuple(wfb) if isinstance(wfb, list) else (wfb,)),
                        sem="dW" if buf == 0 else "dWb")
                    for ht in range(16):
                        K.t(lambda e, ht=ht, wv=wv: e.matmul(
                            ps_x1[0:1, 0:128], ones128b[:, 0:1], wv[:, ht, :],
                            start=(ht == 0), stop=(ht == 15)),
                            waits=((w_ev[buf], aux_ev[0]) if ht == 0 else ()))
                    ev_ws_mm = K.t(lambda e: e.drain(), sig=True)
                    ev_ws = K.v(lambda e: e.tensor_copy(
                        out=r_ws[0:1, :], in_=ps_x1[0:1, 0:128]),
                        waits=(ev_ws_mm,), sig=True)
                    aux_ev[0] = ev_ws
                    if ot == 0 and "d_rws0" in dbg:
                        K.dma("gpsimd", dbg["d_rws0"], r_ws[0:1, :],
                              waits=(ev_ws,))
                        if "d_ps0" in dbg:
                            ev_rws_v2 = K.v(lambda e: e.tensor_copy(
                                out=r_mkb[0:1, 0:64], in_=ps_x1[0:1, 0:64]),
                                waits=(ev_ws,), sig=True)
                            K.dma("gpsimd", dbg["d_ps0"],
                                  r_mkb[0:1, 0:64], waits=(ev_rws_v2,))
                    if ot == 0 and "d_w0" in dbg:
                        K.dma("gpsimd", dbg["d_w0"], wbufs[0][:, :],
                              waits=(w_ev[0],))
                    ods = odss[buf]
                    for tc in range(4):
                        tsl = slice(512 * tc, 512 * (tc + 1))
                        bank = tc
                        for ht in range(16):
                            K.t(lambda e, b=bank, ht=ht, wv=wv, tsl=tsl:
                                e.matmul(
                                    ps_q[b][:, :], wv[:, ht, :],
                                    blkA[ht][:, tsl],
                                    start=(ht == 0), stop=False),
                                waits=((w_ev[buf], evac_ev[bank], ev_rows1)
                                       if ht == 0 else ()))
                        K.t(lambda e, b=bank, tsl=tsl: e.matmul(
                            ps_q[b][:, :], r_ws[0:1, :], r_nm1b[0:1, tsl],
                            start=False, stop=True),
                            waits=(ev_ws, ev_nm1b_done))
                        mm_ev = K.t(lambda e: e.drain(), sig=True)
                        evac_ev[bank] = K.v(
                            lambda e, b=bank, tsl=tsl, ods=ods, tc=tc:
                            e.tensor_tensor(
                                out=ods[:, tsl], in0=ps_q[b][:, :],
                                in1=a1bc[tc][:, :], op=ALU.mult),
                            waits=(mm_ev,) + tuple(ods_free[buf]), sig=True)
                        ods_free[buf] = []
                    ev_od_all = tuple(evac_ev)
                    w_free[buf] = evac_ev[3]
                    if ot == 0 and "d_ods0" in dbg:
                        K.dma("gpsimd", dbg["d_ods0"], ods[:, :],
                              waits=ev_od_all)
                    if ot == 3 and "d_ods3" in dbg:
                        K.dma("gpsimd", dbg["d_ods3"], ods[:, :],
                              waits=ev_od_all)
                    if ot == 3 and "d_rws3" in dbg:
                        K.dma("gpsimd", dbg["d_rws3"], r_ws[0:1, :],
                              waits=(ev_ws,))

                    if part == 0:
                        K.dma("scalar", qscr[128 * head:128 * (head + 1), :],
                              ods[:, :], waits=ev_od_all)
                    elif part == 1:
                        for bb in range(4):
                            g = bb * 16 + head
                            K.dma("scalar",
                                  agin[KOFF + 128 * g:KOFF + 128 * (g + 1),
                                       :],
                                  ods[:, 512 * bb:512 * (bb + 1)],
                                  waits=ev_od_all if bb == 0 else ())

                    if part in (0, 1):
                        ev_sq2 = K.s(lambda e, ods=ods: e.activation(
                            out=sqb2[:, :], in_=ods[:, :], func=AF.Square),
                            waits=ev_od_all + (ev_nrm_mm_last,), sig=True)
                        ev_nc = None
                        for bb in range(4):
                            pa = [ps_x1, ps_x2][bb % 2]
                            K.t(lambda e, bb=bb, pa=pa: e.matmul(
                                pa[0:1, :], ones128b[:, 0:1],
                                sqb2[:, 512 * bb:512 * (bb + 1)],
                                start=True, stop=True),
                                waits=(ev_sq2, aux_ev[bb % 2]))
                            ev_nmm = K.t(lambda e: e.drain(), sig=True)
                            ev_nc = K.v(lambda e, bb=bb, pa=pa: e.tensor_copy(
                                out=r_nrm[0:1, 512 * bb:512 * (bb + 1)],
                                in_=pa[0:1, :]), waits=(ev_nmm,), sig=True)
                            aux_ev[bb % 2] = ev_nc
                            ev_nrm_mm_last = ev_nmm
                        tgt = r_mq if part == 0 else r_mkl
                        ev_mx = None
                        for bb in range(4):
                            g = bb * 16 + head
                            ev_mx = K.v(lambda e, g=g, bb=bb, tgt=tgt:
                                        e.reduce_max(
                                out=tgt[0:1, g:g + 1],
                                in_=r_nrm[0:1, 512 * bb:512 * (bb + 1)],
                                axis=AX.X), waits=(ev_nc,), sig=(bb == 3))
                        ods_free[buf] = [ev_sq2, K.dma_ev("scalar")]
                        if part == 0:
                            ev_mq_done = ev_mx
                        else:
                            ev_mk_done = ev_mx
                    else:
                        vts = vtss[(ot // 3) % 2]
                        ev_ev_all = []
                        for s16 in range(16):
                            pp = tr_ps[s16 % 2]
                            K.t(lambda e, s16=s16, pp=pp, ods=ods:
                                e.transpose(
                                pp[:, :], ods[:, 128 * s16:128 * (s16 + 1)],
                                ident[:, :]),
                                waits=(tr_free[s16 % 2],) + (
                                    ev_od_all if s16 == 0 else ()))
                            ev_tr = K.t(lambda e: e.drain(), sig=True)
                            ev_ev = K.v(lambda e, s16=s16, pp=pp, vts=vts:
                                        e.tensor_copy(
                                out=vts[:, 128 * s16:128 * (s16 + 1)],
                                in_=pp[:, :]),
                                waits=(ev_tr,) + tuple(
                                    vts_free[(ot // 3) % 2]), sig=True)
                            vts_free[(ot // 3) % 2] = []
                            tr_free[s16 % 2] = ev_ev
                            ev_ev_all.append(ev_ev)
                        for bb in range(4):
                            g = bb * 16 + head
                            K.dma("scalar",
                                  agin[VOFF + 128 * g:VOFF + 128 * (g + 1),
                                       :],
                                  vts[:, 512 * bb:512 * (bb + 1)],
                                  waits=(ev_ev_all[4 * bb + 3],))
                        vts_free[(ot // 3) % 2] = [ev_ev_all[-1],
                                                   K.dma_ev("scalar")]
                        ods_free[buf] = [ev_ev_all[-1]]
                        if ot == 2 and "d_vts" in dbg:
                            K.dma("gpsimd", dbg["d_vts"], vts[:, :],
                                  waits=(ev_ev_all[-1],))
                        if ot == 2 and "d_ods2" in dbg:
                            K.dma("gpsimd", dbg["d_ods2"], ods[:, :],
                                  waits=ev_od_all)

                    if oi_seq in dq_at:
                        emit_dequant(*dq_at[oi_seq])
                    if oi_seq == 32:
                        ev_mkb0 = K.s(lambda e: e.nop(),
                                      waits=(ev_mk_done,), sig=True)
                        ev_mkb1 = K.v(lambda e: e.tensor_scalar(
                            out=r_mkb[0:1, :], in0=r_mkl[0:1, :],
                            scalar1=1.0157, scalar2=None, op0=ALU.mult),
                            waits=(ev_mkb0,), sig=True)
                        K.dma("scalar", agin[NOFF:NOFF + 1, 0:64],
                              r_mkb[0:1, :], waits=(ev_mkb1,))
                        ev_cc1 = K.coll(lambda e: e.collective_compute(
                            "AllGather", ALU.bypass,
                            replica_groups=[list(range(NCORES))],
                            ins=[agin], outs=[agout]),
                            waits=(K.dma_ev("scalar"), aux_ev[0],
                                   aux_ev[1]))

                p1_tail = (ev_mq_done, aux_ev[0], aux_ev[1],
                           tr_free[0], tr_free[1], evac_ev[3],
                           ev_nrm_mm_last)
                ev_qscr_all = K.dma_ev("scalar")

            K.enabled = KSTAGE >= 3
            ev_cc2 = K.coll(lambda e: e.collective_compute(
                "AllGather", ALU.bypass,
                replica_groups=[list(range(NCORES))],
                ins=[agin2], outs=[agout2]),
                waits=K.all_dma_evs() + p1_tail)
            ev_cc = ev_cc1

        if "d_agin" in dbg:
            K.dma("gpsimd", dbg["d_agin"], agin, waits=(ev_cc,))
        if "d_qscr" in dbg:
            K.dma("gpsimd", dbg["d_qscr"], qscr, waits=(ev_cc,))

        K.enabled = KSTAGE >= 4
        _sc3 = ExitStack()
        with _sc3 as _s3:
            p_sb = _s3.enter_context(nc.sbuf_tensor("p_sb", [128, 32 * 512], BF16))
            k_sb0 = _s3.enter_context(nc.sbuf_tensor("k_sb0", [128, 4096], BF16))
            k_sb1 = _s3.enter_context(nc.sbuf_tensor("k_sb1", [128, 4096], BF16))
            v_sb0 = _s3.enter_context(nc.sbuf_tensor("v_sb0", [128, 4096], BF16))
            v_sb1 = _s3.enter_context(nc.sbuf_tensor("v_sb1", [128, 4096], BF16))
            q_sb0 = _s3.enter_context(nc.sbuf_tensor("q_sb0", [128, 512], BF16))
            q_sb1 = _s3.enter_context(nc.sbuf_tensor("q_sb1", [128, 512], BF16))
            mk8 = _s3.enter_context(nc.sbuf_tensor("mk8", [1, 512], BF16))
            r_mk2 = _s3.enter_context(nc.sbuf_tensor("r_mk2", [1, 64], F32))
            r_bneg = _s3.enter_context(nc.sbuf_tensor("r_bneg", [1, 64], BF16))
            r_z = _s3.enter_context(nc.sbuf_tensor("r_z", [1, 512], F32))
            r_zib = _s3.enter_context(nc.sbuf_tensor("r_zib", [1, 512], BF16))
            bias_sb = _s3.enter_context(nc.sbuf_tensor("bias_sb", [128, 1], F32))
            zbc_sb = _s3.enter_context(nc.sbuf_tensor("zbc_sb", [128, 512], F32))
            ps_sA = _s3.enter_context(nc.psum_tensor("ps_sA", [128, 1024], F32))
            ps_sB = _s3.enter_context(nc.psum_tensor("ps_sB", [128, 1024], F32))
            ps_ctx = _s3.enter_context(nc.psum_tensor("ps_ctx", [128, 512], F32))
            ps_z = _s3.enter_context(nc.psum_tensor("ps_z", [1, 512], F32))
            ps_bA = _s3.enter_context(nc.psum_tensor("ps_bA", [128, 512], F32))
            ps_bB = _s3.enter_context(nc.psum_tensor("ps_bB", [128, 512], F32))
            ev_mk_ld = None
            for rk in range(NCORES):
                ev_mk_ld = K.dma(
                    "sync", mk8[0:1, 64 * rk:64 * (rk + 1)],
                    agout[rk * RT1 + NOFF:rk * RT1 + NOFF + 1, 0:64],
                    waits=((ev_cc,) + p1_tail + (ev_qscr_all,))
                    if rk == 0 else (ev_cc,), sem="dMK")
            ev_m = K.v(lambda e: e.tensor_tensor(
                out=r_mk2[0:1, :], in0=mk8[0:1, 0:64], in1=mk8[0:1, 64:128],
                op=ALU.max), waits=(ev_mk_ld,), sig=True)
            for rk in range(2, NCORES):
                ev_mS = K.s(lambda e: e.nop(), waits=(ev_m,), sig=True)
                ev_m = K.v(lambda e, rk=rk: e.tensor_tensor(
                    out=r_mk2[0:1, :], in0=r_mk2[0:1, :],
                    in1=mk8[0:1, 64 * rk:64 * (rk + 1)], op=ALU.max),
                    waits=(ev_mS,), sig=True)
            ev_mqS = K.s(lambda e: e.nop(),
                         waits=(ev_mq_done,), sig=True)
            ev_b1 = K.v(lambda e: e.tensor_tensor(
                out=r_mk2[0:1, :], in0=r_mk2[0:1, :], in1=r_mq[0:1, :],
                op=ALU.mult), waits=(ev_m, ev_mqS), sig=True)
            ev_b2 = K.s(lambda e: e.activation(
                out=r_mk2[0:1, :], in_=r_mk2[0:1, :], func=AF.Sqrt),
                waits=(ev_b1,), sig=True)
            ev_b3 = K.v(lambda e: e.tensor_scalar(
                out=r_bneg[0:1, :], in0=r_mk2[0:1, :], scalar1=-RSQ,
                scalar2=None, op0=ALU.mult), waits=(ev_b2,), sig=True)

            kbufs, vbufs, qbufs = [k_sb0, k_sb1], [v_sb0, v_sb1], \
                [q_sb0, q_sb1]
            k_ev = [None, None]
            v_ev = [None, None]
            q_ev = [None, None]
            k_free = [None, None]
            v_free = [None, None]
            q_free = [None, None]

            def load_g(g, buf):
                head, bb = g % 16, g // 16
                for rk in range(NCORES):
                    k_ev[buf] = K.dma(
                        "sync", kbufs[buf][:, 512 * rk:512 * (rk + 1)],
                        agout[rk * RT1 + KOFF + 128 * g:
                              rk * RT1 + KOFF + 128 * (g + 1), :],
                        waits=(ev_cc, k_free[buf]) if rk == 0 else (),
                        sem="dKV" if buf == 0 else "dKVb")
                    v_ev[buf] = K.dma(
                        "sync", vbufs[buf][:, 512 * rk:512 * (rk + 1)],
                        agout[rk * RT1 + VOFF + 128 * g:
                              rk * RT1 + VOFF + 128 * (g + 1), :],
                        waits=(v_free[buf],) if rk == 0 else (),
                        sem="dKV" if buf == 0 else "dKVb")
                q_ev[buf] = K.dma(
                    "sync", qbufs[buf][:, :],
                    qscr[128 * head:128 * (head + 1),
                         512 * bb:512 * (bb + 1)],
                    waits=(q_free[buf], ev_qscr_all),
                    sem="dKV" if buf == 0 else "dKVb")

            load_g(0, 0)
            sgrp_free = [None, None]
            ev_bias = None
            ev_zbc = None
            ev_ctx_evac = None
            ev_z_rd = None
            last_ctx_ev = None
            for g in range(NG):
                buf = g % 2
                n_tile, bbs = g // 4, g % 4
                if g + 1 < NG:
                    load_g(g + 1, 1 - buf)
                ev_bm = K.t(lambda e, g=g: e.matmul(
                    ps_bA[:, 0:1], ones1b[0:1, :], r_bneg[0:1, g:g + 1],
                    start=True, stop=True),
                    waits=(ev_b3, ev_bias), sig=True)
                ev_bias = K.s(lambda e: e.copy(
                    out=bias_sb[:, 0:1], in_=ps_bA[:, 0:1]),
                    waits=(ev_bm,), sig=True)
                exp_ev = [None] * 16
                for r in range(16):
                    grp = r % 2
                    pg = [ps_sA, ps_sB][grp]
                    ev_smm = None
                    for j in range(2):
                        kt = 2 * r + j
                        ev_smm = K.t(lambda e, kt=kt, pg=pg, j=j, buf=buf:
                                     e.matmul(
                            pg[:, 512 * j:512 * (j + 1)],
                            kbufs[buf][:, 128 * kt:128 * (kt + 1)],
                            qbufs[buf][:, :], start=True, stop=True),
                            waits=(((k_ev[buf], q_ev[buf],
                                     sgrp_free[grp]) if r < 2 else
                                    (sgrp_free[grp],)) if j == 0 else ()),
                            sig=(j == 1))
                    exp_ev[r] = K.s(lambda e, r=r, pg=pg: e.activation(
                        out=p_sb[:, 1024 * r:1024 * (r + 1)], in_=pg[:, :],
                        func=AF.Exp, bias=bias_sb[:, 0:1], scale=RSQ),
                        waits=(ev_smm, ev_bias), sig=True)
                    sgrp_free[grp] = exp_ev[r]
                    if r > 0:
                        for j in range(2):
                            kt = 2 * (r - 1) + j
                            K.t(lambda e, kt=kt: e.matmul(
                                ps_z[0:1, :], ones128b[:, 0:1],
                                p_sb[:, 512 * kt:512 * (kt + 1)],
                                start=(kt == 0), stop=False),
                                waits=(exp_ev[r - 1], ev_z_rd)
                                if j == 0 else ())
                            K.t(lambda e, kt=kt, buf=buf: e.matmul(
                                ps_ctx[:, :],
                                vbufs[buf][:, 128 * kt:128 * (kt + 1)],
                                p_sb[:, 512 * kt:512 * (kt + 1)],
                                start=(kt == 0), stop=False),
                                waits=(v_ev[buf], ev_ctx_evac)
                                if kt == 0 else ())
                ev_zmm = None
                ev_pmm = None
                for j in range(2):
                    kt = 30 + j
                    ev_zmm = K.t(lambda e, kt=kt: e.matmul(
                        ps_z[0:1, :], ones128b[:, 0:1],
                        p_sb[:, 512 * kt:512 * (kt + 1)],
                        start=False, stop=(kt == 31)),
                        waits=(exp_ev[15],) if j == 0 else (),
                        sig=(kt == 31))
                    ev_pmm = K.t(lambda e, kt=kt, buf=buf: e.matmul(
                        ps_ctx[:, :],
                        vbufs[buf][:, 128 * kt:128 * (kt + 1)],
                        p_sb[:, 512 * kt:512 * (kt + 1)],
                        start=False, stop=(kt == 31)),
                        sig=(kt == 31))
                k_free[buf] = ev_zmm
                q_free[buf] = ev_zmm
                v_free[buf] = ev_pmm
                ev_z1 = K.v(lambda e: e.tensor_copy(
                    out=r_z[0:1, :], in_=ps_z[0:1, :]),
                    waits=(ev_zmm,), sig=True)
                ev_z_rd = ev_z1
                ev_z1S = K.s(lambda e: e.nop(), waits=(ev_z1,), sig=True)
                ev_z2 = K.v(lambda e: e.reciprocal(
                    out=r_z[0:1, :], in_=r_z[0:1, :]), waits=(ev_z1S,),
                    sig=True)
                ev_zm = K.t(lambda e: e.matmul(
                    ps_bB[:, :], ones1f[0:1, :], r_z[0:1, :],
                    start=True, stop=True),
                    waits=(ev_z2, ev_zbc), sig=True)
                ev_zbc = K.s(lambda e: e.copy(
                    out=zbc_sb[:, :], in_=ps_bB[:, :]),
                    waits=(ev_zm, ev_ctx_evac), sig=True)
                ev_ctx_evac = K.v(
                    lambda e, n_tile=n_tile, bbs=bbs: e.tensor_tensor(
                        out=blkA[n_tile][:, 512 * bbs:512 * (bbs + 1)],
                        in0=ps_ctx[:, :], in1=zbc_sb[:, :], op=ALU.mult),
                    waits=(ev_pmm, ev_zbc), sig=True)
                last_ctx_ev = ev_ctx_evac

        if "d_rhs" in dbg:
            for n in range(16):
                K.dma("gpsimd", dbg["d_rhs"][128 * n:128 * (n + 1), :],
                      blkA[n][:, :], waits=(last_ctx_ev,))

        K.enabled = KSTAGE >= 5
        _sc4 = ExitStack()
        with _sc4 as _s4:
            wp0 = _s4.enter_context(nc.sbuf_tensor("wp0", [128, 16 * 128], BF16))
            wp1 = _s4.enter_context(nc.sbuf_tensor("wp1", [128, 16 * 128], BF16))
            out1 = _s4.enter_context(nc.sbuf_tensor("out1", [128, 16 * 512], F32))
            o1b = _s4.enter_context(nc.sbuf_tensor("o1b", [128, 512], BF16))
            sqb3 = _s4.enter_context(nc.sbuf_tensor("sqb3", [128, 512], BF16))
            bp_sb = _s4.enter_context(nc.sbuf_tensor("bp_sb", [128, 16], F32))
            x1t0 = _s4.enter_context(nc.sbuf_tensor("x1t0", [128, 512], F32))
            x1t1 = _s4.enter_context(nc.sbuf_tensor("x1t1", [128, 512], F32))
            x2t0 = _s4.enter_context(nc.sbuf_tensor("x2t0", [128, 512], F32))
            x2t1 = _s4.enter_context(nc.sbuf_tensor("x2t1", [128, 512], F32))
            x2b = _s4.enter_context(nc.sbuf_tensor("x2b", [128, 512], BF16))
            r_p1 = _s4.enter_context(nc.sbuf_tensor("r_p1", [1, 512], F32))
            r_p2 = _s4.enter_context(nc.sbuf_tensor("r_p2", [1, 512], F32))
            r_p1b = _s4.enter_context(nc.sbuf_tensor("r_p1b", [1, 512], BF16))
            r_p2b = _s4.enter_context(nc.sbuf_tensor("r_p2b", [1, 512], BF16))
            abc_sb = _s4.enter_context(nc.sbuf_tensor("abc_sb", [128, 512], F32))
            nbc_sb = _s4.enter_context(nc.sbuf_tensor("nbc_sb", [128, 512], F32))
            ps_p0 = _s4.enter_context(nc.psum_tensor("ps_p0", [128, 512], F32))
            ps_p1 = _s4.enter_context(nc.psum_tensor("ps_p1", [128, 512], F32))
            ps_p2 = _s4.enter_context(nc.psum_tensor("ps_p2", [128, 512], F32))
            ps_p3 = _s4.enter_context(nc.psum_tensor("ps_p3", [128, 512], F32))
            ps_s2 = _s4.enter_context(nc.psum_tensor("ps_s2", [33, 512], F32))
            ps_s3 = _s4.enter_context(nc.psum_tensor("ps_s3", [33, 512], F32))
            ps_b3 = _s4.enter_context(nc.psum_tensor("ps_b3", [128, 1024], F32))
            ps_p = [ps_p0, ps_p1, ps_p2, ps_p3]
            gate3 = (last_ctx_ev,)
            ev_bp = K.dma("sync", bp_sb[:, :], ap_bp, waits=gate3, sem="dMK")
            wq_p = ap_Wp.rearrange("(t p) n -> p t n", p=128)
            wpb = [wp0, wp1]
            wp_ev = [None, None]
            wp_free = [None, None]
            pevac = [None, None, None, None]
            out1_v = out1.rearrange("p (t n) -> p t n", t=16)
            x2scr_v = x2scr.rearrange("(t p) n -> p t n", p=128)
            xT_v2 = ap_xT.rearrange("(t p) n -> p t n", p=128)
            out1_free = [None] * 16
            x1_free = [None, None]
            x2dma = [None, None]
            ev_sqmm = None
            ev_o1b_mm = None
            ev_ln3_mm = None
            ev_bc3 = None
            ev_s2_rd = None
            ev_s3_rd = None
            ev_a3 = None
            ev_nm3 = None
            for tc in range(4):
                tsl = slice(512 * tc, 512 * (tc + 1))
                for ot in range(16):
                    buf = ot % 2
                    wp_ev[buf] = K.dma(
                        "gpsimd",
                        wpb[buf].rearrange("p (t n) -> p t n", t=16)[:, :, :],
                        wq_p[:, :, 128 * ot:128 * (ot + 1)],
                        waits=(wp_free[buf],) + gate3,
                        sem="dW" if buf == 0 else "dWb")
                    wv = wpb[buf].rearrange("p (t n) -> p t n", t=16)
                    bank = ot % 4
                    ev_mm = None
                    for it in range(16):
                        ev_mm = K.t(lambda e, b=bank, it=it, wv=wv, tsl=tsl:
                                    e.matmul(
                            ps_p[b][:, :], wv[:, it, :], blkA[it][:, tsl],
                            start=(it == 0), stop=(it == 15)),
                            waits=((wp_ev[buf], pevac[bank], last_ctx_ev)
                                   if it == 0 else ()), sig=(it == 15))
                    wp_free[buf] = ev_mm
                    ev_e1 = K.s(lambda e, b=bank, ot=ot: e.add(
                        out=out1_v[:, ot, :], in_=ps_p[b][:, :],
                        add=bp_sb[:, ot:ot + 1]),
                        waits=(ev_mm, ev_bp, out1_free[ot]), sig=True)
                    ev_sq = K.s(lambda e, b=bank, ot=ot: e.activation(
                        out=sqb3[:, :], in_=ps_p[b][:, :], func=AF.Square,
                        bias=bp_sb[:, ot:ot + 1], scale=1.0),
                        waits=(ev_sqmm,), sig=True)
                    ev_o1b = K.v(lambda e, ot=ot: e.tensor_copy(
                        out=o1b[:, :], in_=out1_v[:, ot, :]),
                        waits=(ev_e1, ev_o1b_mm), sig=True)
                    ev_o1b_mm = K.t(lambda e, ot=ot: e.matmul(
                        ps_s2[0:1, :], ones128b[:, 0:1], o1b[:, :],
                        start=(ot == 0), stop=(ot == 15)),
                        waits=(ev_o1b, ev_s2_rd), sig=True)
                    ev_sqmm = K.t(lambda e, ot=ot: e.matmul(
                        ps_s2[32:33, :], ones128b[:, 0:1], sqb3[:, :],
                        start=(ot == 0), stop=(ot == 15)),
                        waits=(ev_sq,), sig=True)
                    pevac[bank] = ev_sq
                ev_r1 = K.v(lambda e: e.tensor_scalar(
                    out=r_p1[0:1, :], in0=ps_s2[0:1, :], scalar1=1.0 / H,
                    scalar2=None, op0=ALU.mult), waits=(ev_sqmm,), sig=True)
                ev_r1S = K.s(lambda e: e.nop(), waits=(ev_r1,), sig=True)
                ev_r2 = K.v(lambda e: e.scalar_tensor_tensor(
                    out=r_p2[0:1, :], in0=r_p1[0:1, :], scalar=-1.0,
                    in1=r_p1[0:1, :], op0=ALU.mult, op1=ALU.mult),
                    waits=(ev_r1S,), sig=True)
                ev_s2_rd = K.v(lambda e: e.scalar_tensor_tensor(
                    out=r_p2[0:1, :], in0=ps_s2[32:33, :], scalar=1.0 / H,
                    in1=r_p2[0:1, :], op0=ALU.mult, op1=ALU.add),
                    waits=(ev_r2,), sig=True)
                ev_r4 = K.s(lambda e: e.activation(
                    out=r_p2[0:1, :], in_=r_p2[0:1, :], func=AF.Ln,
                    bias=cst_eps[0:1, 0:1], scale=1.0), waits=(ev_s2_rd,),
                    sig=True)
                ev_r4V = K.v(lambda e: e.engine_nop(),
                             waits=(ev_r4,), sig=True)
                ev_r5 = K.s(lambda e: e.activation(
                    out=r_p2[0:1, :], in_=r_p2[0:1, :], func=AF.Exp,
                    scale=-0.5), waits=(ev_r4V,), sig=True)
                ev_r6 = K.v(lambda e: e.tensor_scalar(
                    out=r_p1[0:1, :], in0=r_p1[0:1, :], scalar1=-1.0,
                    scalar2=None, op0=ALU.mult), waits=(ev_r1S,), sig=True)
                ev_bm1 = K.t(lambda e: e.matmul(
                    ps_b3[:, 0:512], ones1f[0:1, :], r_p2[0:1, :],
                    start=True, stop=True), waits=(ev_r5, ev_bc3), sig=True)
                ev_bm2 = K.t(lambda e: e.matmul(
                    ps_b3[:, 512:1024], ones1f[0:1, :], r_p1[0:1, :],
                    start=True, stop=True), waits=(ev_r6,), sig=True)
                ev_c1 = K.s(lambda e: e.copy(
                    out=abc_sb[:, :], in_=ps_b3[:, 0:512]),
                    waits=(ev_bm1,), sig=True)
                ev_bc3 = K.s(lambda e: e.copy(
                    out=nbc_sb[:, :], in_=ps_b3[:, 512:1024]),
                    waits=(ev_bm2,), sig=True)
                ev_x2sq = None
                ev_x2b_prev = None
                for ot in range(16):
                    buf = ot % 2
                    x1t = [x1t0, x1t1][buf]
                    x2t = [x2t0, x2t1][buf]
                    ev_x1 = K.dma("sync", x1t[:, :], xT_v2[:, ot, tsl],
                                  waits=(x1_free[buf],) + gate3,
                                  sem="dX" if buf == 0 else "dXb")
                    ev_t1 = K.v(lambda e, ot=ot, x2t=x2t: e.tensor_tensor(
                        out=x2t[:, :], in0=out1_v[:, ot, :], in1=nbc_sb[:, :],
                        op=ALU.add),
                        waits=(ev_bc3, x2dma[buf], ev_x2sq), sig=True)
                    out1_free[ot] = ev_t1
                    ev_t2 = K.v(lambda e, x2t=x2t: e.tensor_tensor(
                        out=x2t[:, :], in0=x2t[:, :], in1=abc_sb[:, :],
                        op=ALU.mult), waits=(ev_c1,), sig=True)
                    ev_t3 = K.v(lambda e, x2t=x2t, x1t=x1t: e.tensor_tensor(
                        out=x2t[:, :], in0=x2t[:, :], in1=x1t[:, :],
                        op=ALU.add), waits=(ev_x1,), sig=True)
                    x1_free[buf] = ev_t3
                    x2dma[buf] = K.dma(
                        "gpsimd", x2scr_v[:, ot, tsl], x2t[:, :],
                        waits=(ev_t3,))
                    ev_x2b = K.v(lambda e, x2t=x2t: e.tensor_copy(
                        out=x2b[:, :], in_=x2t[:, :]),
                        waits=(ev_x2b_prev,), sig=True)
                    ev_x2sq = K.s(lambda e, x2t=x2t: e.activation(
                        out=sqb3[:, :], in_=x2t[:, :], func=AF.Square),
                        waits=(ev_ln3_mm, ev_t3), sig=True)
                    ev_x2b_prev = K.t(lambda e, ot=ot: e.matmul(
                        ps_s3[0:1, :], ones128b[:, 0:1], x2b[:, :],
                        start=(ot == 0), stop=(ot == 15)),
                        waits=(ev_x2b, ev_s3_rd), sig=True)
                    ev_ln3_mm = K.t(lambda e, ot=ot: e.matmul(
                        ps_s3[32:33, :], ones128b[:, 0:1], sqb3[:, :],
                        start=(ot == 0), stop=(ot == 15)),
                        waits=(ev_x2sq,), sig=True)
                ev_q1 = K.v(lambda e: e.tensor_scalar(
                    out=r_p1[0:1, :], in0=ps_s3[0:1, :], scalar1=1.0 / H,
                    scalar2=None, op0=ALU.mult), waits=(ev_ln3_mm,), sig=True)
                ev_q1S = K.s(lambda e: e.nop(), waits=(ev_q1,), sig=True)
                ev_q2 = K.v(lambda e: e.scalar_tensor_tensor(
                    out=r_p2[0:1, :], in0=r_p1[0:1, :], scalar=-1.0,
                    in1=r_p1[0:1, :], op0=ALU.mult, op1=ALU.mult),
                    waits=(ev_q1S,), sig=True)
                ev_s3_rd = K.v(lambda e: e.scalar_tensor_tensor(
                    out=r_p2[0:1, :], in0=ps_s3[32:33, :], scalar=1.0 / H,
                    in1=r_p2[0:1, :], op0=ALU.mult, op1=ALU.add),
                    waits=(ev_q2,), sig=True)
                ev_q4 = K.s(lambda e: e.activation(
                    out=r_p2[0:1, :], in_=r_p2[0:1, :], func=AF.Ln,
                    bias=cst_eps[0:1, 0:1], scale=1.0), waits=(ev_s3_rd,),
                    sig=True)
                ev_q4V = K.v(lambda e: e.engine_nop(),
                             waits=(ev_q4,), sig=True)
                ev_q5 = K.s(lambda e, tsl=tsl: e.activation(
                    out=r_a3f[0:1, tsl], in_=r_p2[0:1, :], func=AF.Exp,
                    scale=-0.5), waits=(ev_q4V,), sig=True)
                ev_a3 = ev_q5
                ev_q6 = K.v(lambda e: e.tensor_scalar(
                    out=r_p1[0:1, :], in0=r_p1[0:1, :], scalar1=-1.0,
                    scalar2=None, op0=ALU.mult), waits=(ev_q1S,), sig=True)
                ev_nm3 = K.v(lambda e, tsl=tsl: e.tensor_copy(
                    out=r_nm3b[0:1, tsl], in_=r_p1[0:1, :]),
                    waits=(ev_q6,), sig=True)
            gate4 = (ev_a3, ev_nm3, K.dma_ev("gpsimd"), K.dma_ev("dX"))

        blkA_scope.close()

        if "d_x2" in dbg:
            K.dma("gpsimd", dbg["d_x2"], x2scr, waits=gate4)

        K.enabled = KSTAGE >= 6
        _sc5 = ExitStack()
        with _sc5 as _s5:
            x2bf = _s5.enter_context(nc.sbuf_tensor("x2bf", [128, 16 * 1024], BF16))
            h_sb = _s5.enter_context(nc.sbuf_tensor("h_sb", [128, 44 * 1024], BF16))
            w_sb0 = _s5.enter_context(nc.sbuf_tensor("w_sb0", [128, 2048], BF16))
            w_sb1 = _s5.enter_context(nc.sbuf_tensor("w_sb1", [128, 2048], BF16))
            wsr0 = _s5.enter_context(nc.sbuf_tensor("wsr0", [1, 128], BF16))
            wsr1 = _s5.enter_context(nc.sbuf_tensor("wsr1", [1, 128], BF16))
            dn0 = _s5.enter_context(nc.sbuf_tensor("dn0", [128, 512], BF16))
            dn1 = _s5.enter_context(nc.sbuf_tensor("dn1", [128, 512], BF16))
            g1t = _s5.enter_context(nc.sbuf_tensor("g1", [128, 1024], F32))
            u1t = _s5.enter_context(nc.sbuf_tensor("u1", [128, 1024], F32))
            sgt = _s5.enter_context(nc.sbuf_tensor("sg", [128, 1024], F32))
            ybf = _s5.enter_context(nc.sbuf_tensor("ybf", [128, 1024], BF16))
            yf0 = _s5.enter_context(nc.sbuf_tensor("yf0", [128, 1024], F32))
            yf1 = _s5.enter_context(nc.sbuf_tensor("yf1", [128, 1024], F32))
            sqb4 = _s5.enter_context(nc.sbuf_tensor("sqb4", [128, 1024], BF16))
            r_f1 = _s5.enter_context(nc.sbuf_tensor("r_f1", [1, 1024], F32))
            r_f2 = _s5.enter_context(nc.sbuf_tensor("r_f2", [1, 1024], F32))
            r_f1b = _s5.enter_context(nc.sbuf_tensor("r_f1b", [1, 1024], BF16))
            r_f2b = _s5.enter_context(nc.sbuf_tensor("r_f2b", [1, 1024], BF16))
            a4bc = _s5.enter_context(nc.sbuf_tensor("a4bc", [128, 1024], F32))
            n4bc = _s5.enter_context(nc.sbuf_tensor("n4bc", [128, 1024], F32))
            ot0 = _s5.enter_context(nc.sbuf_tensor("ot0", [128, 1024], F32))
            ot1 = _s5.enter_context(nc.sbuf_tensor("ot1", [128, 1024], F32))
            ps_f0 = _s5.enter_context(nc.psum_tensor("ps_f0", [128, 1024], F32))
            ps_f1 = _s5.enter_context(nc.psum_tensor("ps_f1", [128, 1024], F32))
            ps_y0 = _s5.enter_context(nc.psum_tensor("ps_y0", [33, 512], F32))
            ps_y1 = _s5.enter_context(nc.psum_tensor("ps_y1", [33, 512], F32))
            ps_b4 = _s5.enter_context(nc.psum_tensor("ps_b4", [128, 1024], F32))
            ps_f = [ps_f0, ps_f1]
            wsb = [w_sb0, w_sb1]
            wsrb = [wsr0, wsr1]
            dnb = [dn0, dn1]
            x2bf_v = x2bf.rearrange("p (t n) -> p t n", t=16)
            h_v = h_sb.rearrange("p (t n) -> p t n", t=44)
            x2scr_v2 = x2scr.rearrange("(t p) n -> p t n", p=128)
            yscr_v = yscr.rearrange("(t p) n -> p t n", p=128)
            out_v = ap_out.rearrange("(t p) n -> p t n", p=128)

            prev_down_done = None
            prev_h_last = None
            prev_stat_rd = None
            ev_b4_free = None
            prev_final_rd = None
            for tc2 in range(2):
                t2sl = slice(1024 * tc2, 1024 * (tc2 + 1))
                ev_x2l = None
                for ht in range(16):
                    ev_x2l = K.dma(
                        "gpsimd", x2bf_v[:, ht, :], x2scr_v2[:, ht, t2sl],
                        waits=(gate4 + (prev_h_last,)) if ht == 0 else (),
                        sem="dW")
                ev_fb1 = K.t(lambda e, tc2=tc2: e.matmul(
                    ps_b4[:, 0:512], ones1f[0:1, :],
                    r_a3f[0:1, 1024 * tc2:1024 * tc2 + 512],
                    start=True, stop=True),
                    waits=(ev_a3, ev_b4_free), sig=True)
                ev_fb2 = K.t(lambda e, tc2=tc2: e.matmul(
                    ps_b4[:, 512:1024], ones1f[0:1, :],
                    r_a3f[0:1, 1024 * tc2 + 512:1024 * (tc2 + 1)],
                    start=True, stop=True), sig=True)
                ev_a3bc = K.s(lambda e: e.copy(
                    out=a4bc[:, :], in_=ps_b4[:, :]),
                    waits=(ev_fb1, ev_fb2, prev_final_rd), sig=True)

                w_free2 = [list(gate4), list(gate4)]
                fevac = [None, None]
                ev_h = None
                ev_sil = None
                for it in range(44):
                    evs_gu = []
                    for half, jo in ((0, it), (1, 44 + it)):
                        rk, oi = jo % 8, jo // 8
                        pbank = half
                        wbuf = half
                        base = rk * RT2 + WOFF + oi * 4 * 128
                        wf2 = w_free2[wbuf]
                        wf2t = tuple(wf2) if isinstance(wf2, list) else (wf2,)
                        ev_ws2 = K.dma(
                            "sync", wsrb[wbuf][0:1, :],
                            agout2[rk * RT2 + WSOFF + oi:
                                   rk * RT2 + WSOFF + oi + 1, 0:128],
                            waits=(ev_cc2,) + wf2t,
                            sem="dWS" if wbuf == 0 else "dWSb")
                        ev_wl = [None] * 4
                        for hg in range(4):
                            ev_wl[hg] = K.dma(
                                "sync",
                                wsb[wbuf][:, 512 * hg:512 * (hg + 1)],
                                agout2[base + hg * 128:
                                       base + (hg + 1) * 128, :],
                                waits=wf2t if hg == 0 else (),
                                sem="dWB" if wbuf == 0 else "dWBb")
                        ev_mm = None
                        for hg in range(4):
                            for hs in range(4):
                                ht = hg * 4 + hs
                                for nn in range(2):
                                    ev_mm = K.t(
                                        lambda e, pb=pbank, ht=ht, hg=hg,
                                        hs=hs, nn=nn, wbuf=wbuf: e.matmul(
                                            ps_f[pb][:, 512 * nn:
                                                     512 * (nn + 1)],
                                            wsb[wbuf][:, 512 * hg + 128 * hs:
                                                      512 * hg +
                                                      128 * (hs + 1)],
                                            x2bf_v[:, ht, 512 * nn:
                                                   512 * (nn + 1)],
                                            start=(ht == 0), stop=False),
                                        waits=((ev_wl[hg], fevac[pbank],
                                                ev_x2l, prev_down_done)
                                               if hs == 0 and nn == 0
                                               else ()))
                        ev_f = None
                        for nn in range(2):
                            ev_f = K.t(lambda e, pb=pbank, nn=nn, wbuf=wbuf,
                                       tc2=tc2: e.matmul(
                                ps_f[pb][:, 512 * nn:512 * (nn + 1)],
                                wsrb[wbuf][0:1, :],
                                r_nm3b[0:1, 1024 * tc2 + 512 * nn:
                                       1024 * tc2 + 512 * (nn + 1)],
                                start=False, stop=True),
                                waits=(ev_nm3,) if nn == 0 else (),
                                sig=(nn == 1))
                        w_free2[wbuf] = ev_f
                        evs_gu.append(ev_f)
                    ev_g1 = K.v(lambda e: e.tensor_tensor(
                        out=g1t[:, :], in0=ps_f[0][:, :], in1=a4bc[:, :],
                        op=ALU.mult), waits=(evs_gu[0], ev_a3bc, ev_sil),
                        sig=True)
                    ev_u1 = K.v(lambda e: e.tensor_tensor(
                        out=u1t[:, :], in0=ps_f[1][:, :], in1=a4bc[:, :],
                        op=ALU.mult), waits=(evs_gu[1], ev_h), sig=True)
                    fevac[0] = ev_g1
                    fevac[1] = ev_u1
                    ev_sil = K.s(lambda e: e.activation(
                        out=sgt[:, :], in_=g1t[:, :], func=AF.Silu),
                        waits=(ev_g1, ev_h), sig=True)
                    ev_h = K.g(lambda e, it=it: e.tensor_tensor(
                        out=h_v[:, it, :], in0=sgt[:, :], in1=u1t[:, :],
                        op=ALU.mult),
                        waits=(ev_sil, ev_u1, prev_down_done), sig=True)
                prev_h_last = ev_h

                if KSTAGE < 7:
                    continue
                ev_ystat1 = None
                ev_ystat2 = None
                ev_ybf = None
                ev_ysq = None
                dn_ev = [None, None]
                dn_free = [None, None]
                ev_yd = [None, None]
                ev_down_last = None
                for hot in range(16):
                    rk, hc = hot // 2, hot % 2
                    pbank = hot % 2
                    base = rk * RT2 + DOFF + hc * 11 * 128
                    ev_mm = None
                    for ig in range(11):
                        db = ig % 2
                        dn_ev[db] = K.dma(
                            "sync", dnb[db][:, :],
                            agout2[base + ig * 128:base + (ig + 1) * 128, :],
                            waits=(dn_free[db],) + (
                                (prev_h_last,) if hot == 0 and ig == 0
                                else ()), sem="dWB" if db == 0 else "dWBb")
                        for isub in range(4):
                            it = ig * 4 + isub
                            for nn in range(2):
                                ev_mm = K.t(
                                    lambda e, pb=pbank, it=it, isub=isub,
                                    nn=nn, db=db: e.matmul(
                                        ps_f[pb][:, 512 * nn:512 * (nn + 1)],
                                        dnb[db][:, 128 * isub:
                                                128 * (isub + 1)],
                                        h_v[:, it, 512 * nn:512 * (nn + 1)],
                                        start=(it == 0),
                                        stop=(it == 43)),
                                    waits=((dn_ev[db], fevac[pbank],
                                            prev_h_last)
                                           if isub == 0 and nn == 0 else ()),
                                    sig=(isub == 3 and nn == 1))
                        dn_free[db] = ev_mm
                    ev_down_last = ev_mm
                    yt = [yf0, yf1][hot % 2]
                    ev_yf = K.s(lambda e, pb=pbank, yt=yt: e.copy(
                        out=yt[:, :], in_=ps_f[pb][:, :]),
                        waits=(ev_mm, ev_yd[hot % 2], prev_final_rd),
                        sig=True)
                    ev_yd[hot % 2] = K.dma(
                        "gpsimd", yscr_v[:, hot, t2sl], yt[:, :],
                        waits=(ev_yf,), sem="dY" if tc2 == 0 else "dYb")
                    if KSUB < 2:
                        fevac[pbank] = ev_yf
                        continue
                    ev_ybf = K.v(lambda e, yt=yt: e.tensor_copy(
                        out=ybf[:, :], in_=yt[:, :]),
                        waits=(ev_yf, ev_ystat1), sig=True)
                    if KSUB < 3:
                        fevac[pbank] = ev_ybf
                        continue
                    ev_ysq = K.s(lambda e, yt=yt: e.activation(
                        out=sqb4[:, :], in_=yt[:, :], func=AF.Square),
                        waits=(ev_ystat2, ev_yf), sig=True)
                    for nn in range(2):
                        ev_ystat1 = K.t(lambda e, hot=hot, nn=nn: e.matmul(
                            [ps_y0, ps_y1][nn][0:1, :], ones128b[:, 0:1],
                            ybf[:, 512 * nn:512 * (nn + 1)],
                            start=(hot == 0), stop=(hot == 15)),
                            waits=(ev_ybf, prev_stat_rd), sig=True)
                        ev_ystat2 = K.t(lambda e, hot=hot, nn=nn: e.matmul(
                            [ps_y0, ps_y1][nn][32:33, :], ones128b[:, 0:1],
                            sqb4[:, 512 * nn:512 * (nn + 1)],
                            start=(hot == 0), stop=(hot == 15)),
                            waits=(ev_ysq,), sig=True)
                    fevac[pbank] = ev_yf
                prev_down_done = ev_down_last

                if KSTAGE < 8:
                    continue
                ev_e2 = None
                for nn in range(2):
                    nsl = slice(512 * nn, 512 * (nn + 1))
                    pz = [ps_y0, ps_y1][nn]
                    ev_c1m = K.v(lambda e, pz=pz, nsl=nsl: e.tensor_scalar(
                        out=r_f1[0:1, nsl], in0=pz[0:1, :], scalar1=1.0 / H,
                        scalar2=None, op0=ALU.mult),
                        waits=(ev_ystat1, ev_ystat2), sig=True)
                    ev_c1S = K.s(lambda e: e.nop(), waits=(ev_c1m,),
                                 sig=True)
                    ev_c2m = K.v(lambda e, nsl=nsl: e.scalar_tensor_tensor(
                        out=r_f2[0:1, nsl], in0=r_f1[0:1, nsl], scalar=-1.0,
                        in1=r_f1[0:1, nsl], op0=ALU.mult, op1=ALU.mult),
                        waits=(ev_c1S,), sig=True)
                    ev_e2 = K.v(lambda e, pz=pz, nsl=nsl:
                                e.scalar_tensor_tensor(
                        out=r_f2[0:1, nsl], in0=pz[32:33, :],
                        scalar=(1.0 + EPS_LN) / H, in1=r_f2[0:1, nsl],
                        op0=ALU.mult, op1=ALU.add), waits=(ev_c2m,),
                        sig=True)
                prev_stat_rd = ev_e2
                ev_e3 = K.s(lambda e: e.activation(
                    out=r_f2[0:1, :], in_=r_f2[0:1, :], func=AF.Ln,
                    bias=cst_eps2[0:1, 0:1], scale=1.0), waits=(ev_e2,),
                    sig=True)
                ev_e3V = K.v(lambda e: e.engine_nop(),
                             waits=(ev_e3,), sig=True)
                ev_e5 = K.s(lambda e: e.activation(
                    out=r_f2[0:1, :], in_=r_f2[0:1, :], func=AF.Exp,
                    scale=-0.5), waits=(ev_e3V,), sig=True)
                ev_e2S = K.s(lambda e: e.nop(), waits=(ev_e2,), sig=True)
                ev_e7 = K.v(lambda e: e.tensor_scalar(
                    out=r_f1[0:1, :], in0=r_f1[0:1, :], scalar1=-1.0,
                    scalar2=None, op0=ALU.mult), waits=(ev_e2S,), sig=True)
                ev_g1m = K.t(lambda e: e.matmul(
                    ps_b4[:, 0:512], ones1f[0:1, :], r_f2[0:1, 0:512],
                    start=True, stop=True), waits=(ev_e5, ev_a3bc), sig=True)
                ev_g2m = K.t(lambda e: e.matmul(
                    ps_b4[:, 512:1024], ones1f[0:1, :], r_f2[0:1, 512:1024],
                    start=True, stop=True), sig=True)
                ev_a4 = K.s(lambda e: e.copy(
                    out=a4bc[:, :], in_=ps_b4[:, :]),
                    waits=(ev_g1m, ev_g2m), sig=True)
                ev_g3m = K.t(lambda e: e.matmul(
                    ps_b4[:, 0:512], ones1f[0:1, :], r_f1[0:1, 0:512],
                    start=True, stop=True), waits=(ev_e7, ev_a4), sig=True)
                ev_g4m = K.t(lambda e: e.matmul(
                    ps_b4[:, 512:1024], ones1f[0:1, :], r_f1[0:1, 512:1024],
                    start=True, stop=True), sig=True)
                ev_n4 = K.s(lambda e: e.copy(
                    out=n4bc[:, :], in_=ps_b4[:, :]),
                    waits=(ev_g3m, ev_g4m), sig=True)
                ev_b4_free = ev_n4

                dg_after_down = K.dma_ev("dY" if tc2 == 0 else "dYb")
                ev_outd = [None, None]
                ev_f3p = [None, None]
                for ht in range(16):
                    buf = ht % 2
                    yt = [yf0, yf1][buf]
                    x2t = [ot0, ot1][buf]
                    ev_yl = K.dma(
                        "gpsimd", yt[:, :], yscr_v[:, ht, t2sl],
                        waits=(dg_after_down, ev_yf, ev_f3p[buf]),
                        sem="dY2" if buf == 0 else "dY2b")
                    ev_xl = K.dma("sync", x2t[:, :], x2scr_v2[:, ht, t2sl],
                                  waits=(ev_outd[buf], gate4[2]),
                                  sem="dX" if buf == 0 else "dXb")
                    ev_f1 = K.v(lambda e, yt=yt: e.tensor_tensor(
                        out=yt[:, :], in0=yt[:, :], in1=n4bc[:, :],
                        op=ALU.add), waits=(ev_yl, ev_n4), sig=True)
                    ev_f2 = K.v(lambda e, yt=yt: e.tensor_tensor(
                        out=yt[:, :], in0=yt[:, :], in1=a4bc[:, :],
                        op=ALU.mult), waits=(ev_f1,), sig=True)
                    ev_f3 = K.v(lambda e, yt=yt, x2t=x2t: e.tensor_tensor(
                        out=x2t[:, :], in0=x2t[:, :], in1=yt[:, :],
                        op=ALU.add), waits=(ev_f2, ev_xl), sig=True)
                    ev_f3p[buf] = ev_f3
                    prev_final_rd = ev_f3
                    ev_outd[buf] = K.dma(
                        "scalar", out_v[:, ht, t2sl], x2t[:, :],
                        waits=(ev_f3,))

        if "d_y" in dbg:
            K.dma("gpsimd", dbg["d_y"], yscr,
                  waits=(K.dma_ev("dY"), K.dma_ev("dYb")))
        K.enabled = True
        K.wait_only("sync", K.all_dma_evs())

        with nc.Block() as block:
            K.build(block, sems)

    return nc


_NC_CACHE = {}


def kernel(**inputs):
    in_maps = stage_inputs(inputs)
    key = tuple(sorted(n for n, _, _ in DEBUG_TAPS))
    if key not in _NC_CACHE:
        _NC_CACHE[key] = build_program()
    nc = _NC_CACHE[key]
    res = run_bass_kernel_spmd(nc, in_maps, core_ids=list(range(NCORES)))
    kernel.last_results = res.results
    return assemble_output(res.results)
